# version 22 (final)
# baseline (speedup 1.0000x reference)
import contextlib
import numpy as np
import ml_dtypes
import concourse.bass as bass
import concourse.mybir as mybir
from concourse.bass_utils import run_bass_kernel_spmd

F32 = mybir.dt.float32
BF16 = mybir.dt.bfloat16
AF = mybir.ActivationFunctionType
ALU = mybir.AluOpType
AX = mybir.AxisListType

NCORES = 8
SEQ = 2
S = 4096
D = 1024
DC = 8
I = 2048
IC = 16
NE = 8
NH = 16
HD = 64
TP = 256
NT = S // TP
COLS = SEQ * TP
NLAT = 64
LD = 256
NEG = -1e9

_PHASES = ("ffn1", "attn", "ffn2")

ENGINES = ("tensor", "vector", "scalar", "gpsimd", "gpd", "sync")


class Buf:
    __slots__ = ("name", "writer", "readers")

    def __init__(self, name):
        self.name = name
        self.writer = None
        self.readers = []


class FW:

    def __init__(self, nc):
        self.nc = nc
        self.q = {e: [] for e in ENGINES}
        self.cnt = {e: 0 for e in ENGINES}
        self.hw = {e: {f: 0 for f in ENGINES} for e in ENGINES}
        self.sems = {}
        self.n_waits = 0

    def buf(self, name):
        return Buf(name)

    def op(self, eng, fn, reads=(), writes=(), deps=(), dma=False, signal=True):
        inc = 16 if dma else 1
        need = list(deps)
        for b in reads:
            if b.writer is not None:
                need.append(b.writer)
        for b in writes:
            if b.writer is not None:
                need.append(b.writer)
            need.extend(b.readers)
        per = {}
        for (e2, v) in need:
            assert v <= self.cnt[e2] + (0 if e2 != eng else 1), f"future dep {e2}:{v}"
            if e2 == eng and v > self.cnt[e2]:
                continue
            if per.get(e2, 0) < v:
                per[e2] = v
        waits = []
        for e2 in sorted(per):
            v = per[e2]
            if self.hw[eng][e2] >= v:
                continue
            self.hw[eng][e2] = v
            waits.append((e2, v))
        self.n_waits += len(waits)
        if signal:
            tok = (eng, self.cnt[eng] + inc)
            self.cnt[eng] += inc
        else:
            tok = (eng, self.cnt[eng] + 1)

        def closure(engobj, fw=self, waits=tuple(waits), fn=fn, inc=inc, eng=eng, signal=signal):
            for (e2, v) in waits:
                engobj.wait_ge(fw.sems[e2], v)
            ins = fn(engobj)
            if signal:
                ins.then_inc(fw.sems[eng], inc)

        self.q["gpsimd" if eng == "gpd" else eng].append(closure)
        for b in reads:
            b.readers.append(tok)
        for b in writes:
            b.writer = tok
            b.readers = []
        return tok

    def prewait(self, eng, reads=(), writes=(), deps=()):
        need = list(deps)
        for b in reads:
            if b.writer is not None:
                need.append(b.writer)
        for b in writes:
            if b.writer is not None:
                need.append(b.writer)
            need.extend(b.readers)
        per = {}
        for (e2, v) in need:
            if e2 == eng and v > self.cnt[e2]:
                continue
            if per.get(e2, 0) < v:
                per[e2] = v
        waits = []
        for e2 in sorted(per):
            v = per[e2]
            if self.hw[eng][e2] >= v:
                continue
            self.hw[eng][e2] = v
            waits.append((e2, v))
        if not waits:
            return
        self.n_waits += len(waits)

        def closure(engobj, fw=self, waits=tuple(waits)):
            for (e2, v) in waits:
                engobj.wait_ge(fw.sems[e2], v)

        self.q["gpsimd" if eng == "gpd" else eng].append(closure)

    def finalize(self, stack):
        nc = self.nc
        for e in ENGINES:
            self.sems[e] = stack.enter_context(nc.semaphore(f"sem_{e}"))
        block = stack.enter_context(nc.Block())
        fw = self

        def emit(engobj, e):
            for c in fw.q[e]:
                c(engobj)
        assert not fw.q["gpd"]

        @block.tensor
        def _(t):
            emit(t, "tensor")

        @block.vector
        def _(v):
            emit(v, "vector")

        @block.scalar
        def _(s):
            emit(s, "scalar")

        @block.gpsimd
        def _(g):
            emit(g, "gpsimd")
            for e in ENGINES:
                if fw.cnt[e] > 0:
                    g.wait_ge(fw.sems[e], fw.cnt[e])

        @block.sync
        def _(s):
            emit(s, "sync")


_W_SPECS = [
    ("qkv", D, 3 * D),
    ("rq", D, LD),
    ("rkv", D, 2 * LD),
    ("rout", LD, D),
    ("outw", D, D),
    ("wr", D, 8),
    ("gs", D, I), ("us", D, I), ("ds", I, D),
]
for _e in range(NE):
    _W_SPECS += [(f"g{_e}", D, I), (f"u{_e}", D, I), (f"d{_e}", I, D)]


def _w_offsets():
    offs, off = {}, 0
    for name, K, N in _W_SPECS:
        offs[name] = (off, K, N)
        off += (K // 128) * N
    return offs, off


W_OFFS, W_COLS = _w_offsets()

B_OFFS = {"qkv": 0, "rq": 3072, "rkv": 3328, "rout": 3840, "outw": 4864, "rb": 5888}
B_COLS = 5896

C_M256, C_M128, C_LMASK, C_ID, C_GL, C_GR, C_SEL2, C_SELE = 0, 256, 384, 512, 640, 642, 644, 772
C_COLS = 772 + 8 * 128
CF_COLS = 8


def _host_layouts(ins):
    x = np.ascontiguousarray(ins["L0_attn_x"])

    mats = {
        "qkv": np.ascontiguousarray(ins["L0_attn_qkv_w"]),
        "rq": np.ascontiguousarray(ins["L0_attn_rq_w"]),
        "rkv": np.ascontiguousarray(ins["L0_attn_rkv_w"]),
        "rout": np.ascontiguousarray(ins["L0_attn_rout_w"]),
        "outw": np.ascontiguousarray(ins["L0_attn_out_w"]),
        "wr": np.ascontiguousarray(ins["L0_ffn_Wr"].T),
        "gs": np.ascontiguousarray(ins["L0_ffn_Wg_s"].T),
        "us": np.ascontiguousarray(ins["L0_ffn_Wu_s"].T),
        "ds": np.ascontiguousarray(ins["L0_ffn_Wd_s"].T),
    }
    for e in range(NE):
        mats[f"g{e}"] = np.ascontiguousarray(ins["L0_ffn_Wg"][e].T)
        mats[f"u{e}"] = np.ascontiguousarray(ins["L0_ffn_Wu"][e].T)
        mats[f"d{e}"] = np.ascontiguousarray(ins["L0_ffn_Wd"][e].T)

    wf = np.empty((128, W_COLS), np.float32)
    for name, (off, K, N) in W_OFFS.items():
        W = mats[name]
        KC = K // 128
        if N >= 128:
            MC = N // 128
            arr = W.reshape(KC, 128, MC, 128).transpose(1, 2, 0, 3).reshape(128, KC * N)
        else:
            arr = W.reshape(KC, 128, N).transpose(1, 0, 2).reshape(128, KC * N)
        wf[:, off:off + KC * N] = arr

    bia = np.zeros((1, B_COLS), np.float32)
    bia[0, 0:3072] = ins["L0_attn_qkv_b"]
    bia[0, 3072:3328] = ins["L0_attn_rq_b"]
    bia[0, 3328:3840] = ins["L0_attn_rkv_b"]
    bia[0, 3840:4864] = ins["L0_attn_rout_b"]
    bia[0, 4864:5888] = ins["L0_attn_out_b"]
    bia[0, 5888:5896] = ins["L0_ffn_rb"]

    cst = np.zeros((128, C_COLS), np.float32)
    i_ = np.arange(128)[:, None]
    j_ = np.arange(256)[None, :]
    cst[:, C_M256:C_M256 + 256] = np.where((j_ >= i_) & (j_ <= i_ + 127), 0.0, NEG)
    j2 = np.arange(128)[None, :]
    cst[:, C_M128:C_M128 + 128] = np.where(j2 <= i_ - 1, 0.0, NEG)
    lm = np.where(j2 <= 63, 0.0, NEG)
    lm = np.where((i_ >= 64) & (j2 == 64), 0.0, lm)
    cst[:, C_LMASK:C_LMASK + 128] = lm
    cst[:, C_ID:C_ID + 128] = np.eye(128, dtype=np.float32)
    gl = ins["L0_attn_gl_w"].reshape(64)
    gr = ins["L0_attn_gr_w"].reshape(64)
    cst[0:64, C_GL] = gl
    cst[64:128, C_GL + 1] = gl
    cst[0:64, C_GR] = gr
    cst[64:128, C_GR + 1] = gr
    cst[0, C_SEL2:C_SEL2 + 64] = 1.0
    cst[1, C_SEL2 + 64:C_SEL2 + 128] = 1.0
    for e in range(NE):
        cst[e, C_SELE + e * 128:C_SELE + (e + 1) * 128] = 1.0

    cstf = np.zeros((128, CF_COLS), np.float32)
    cstf[:, 0:4] = ins["L0_attn_rkv_b"].reshape(4, 128).T
    cstf[0, 4] = 1e-5

    in_maps = []
    for c in range(NCORES):
        xc = x[SEQ * c:SEQ * (c + 1)]
        arr = xc.reshape(SEQ, NT, TP, DC, 128).transpose(4, 1, 3, 0, 2).reshape(128, NT * DC * COLS)
        in_maps.append({
            "x": np.ascontiguousarray(arr),
            "wf": wf, "bia": bia, "cst": cst, "cstf": cstf,
        })
    return in_maps


def _unshard_output(res):
    outs = []
    for c in range(NCORES):
        y = np.asarray(res.results[c]["y"], np.float32)
        a = y.reshape(128, NT, DC, SEQ, TP).transpose(3, 1, 4, 2, 0).reshape(SEQ, S, D)
        outs.append(a)
    return np.concatenate(outs, 0)


def build_program(phases=None):
    if phases is None:
        phases = _PHASES
    nc = bass.Bass("TRN2", target_bir_lowering=False)
    x_d = nc.dram_tensor("x", [128, NT * DC * COLS], F32, kind="ExternalInput").ap()
    wf_d = nc.dram_tensor("wf", [128, W_COLS], F32, kind="ExternalInput").ap()
    bia_d = nc.dram_tensor("bia", [1, B_COLS], F32, kind="ExternalInput").ap()
    cst_d = nc.dram_tensor("cst", [128, C_COLS], F32, kind="ExternalInput").ap()
    cstf_d = nc.dram_tensor("cstf", [128, CF_COLS], F32, kind="ExternalInput").ap()
    y_d = nc.dram_tensor("y", [128, NT * DC * COLS], F32, kind="ExternalOutput").ap()
    wbf_d = nc.dram_tensor("wbf", [128, W_COLS], BF16, kind="Internal").ap()

    stack = contextlib.ExitStack()
    with stack:
        SB = lambda n, sh, dt: stack.enter_context(nc.sbuf_tensor(n, sh, dt))
        PS = lambda n, sh, dt: stack.enter_context(nc.psum_tensor(n, sh, dt))

        NYB = 2
        NWS = 4
        y = [SB(f"y{j}", [128, DC * COLS], F32) for j in range(NYB)]
        lnA = SB("lnA", [128, DC * COLS], BF16)
        lnB = SB("lnB", [128, DC * COLS], BF16)
        h = SB("h", [128, IC * COLS], BF16)
        h2 = SB("h2", [128, IC * COLS], BF16)
        cwb = SB("cwb", [128, 4 * COLS], BF16)
        sg = [SB(f"sg{j}", [128, COLS], F32) for j in range(2)]
        sqf = SB("sqf", [128, 2 * COLS], BF16)
        scr = [SB(f"scr{j}", [128, COLS], F32) for j in range(2)]
        wring = [SB(f"w{i}", [128, 4096], BF16) for i in range(NWS)]
        qb = SB("qb", [128, DC * COLS], BF16)
        lnF1 = qb
        kb = SB("kb", [128, DC * COLS], BF16)
        ktail = SB("ktail", [128, DC * SEQ * 128], BF16)
        vb = SB("vb", [128, 4 * D], BF16)
        vtail = SB("vtail", [128, SEQ * D], BF16)
        rqb = SB("rqb", [128, 2 * COLS], BF16)
        pooled = SB("pooled", [128, DC * SEQ * NLAT], BF16)
        rkbuf = SB("rkbuf", [128, 2 * 128], BF16)
        rvfbuf = SB("rvfbuf", [128, 2 * 128], BF16)
        rvlat = SB("rvlat", [128, LD], BF16)
        latout = SB("latout", [128, 2 * COLS], BF16)
        pbf = SB("pbf", [128, 4 * 256], BF16)
        attnT = SB("attnT", [128, 4 * 256], BF16)
        lattnT = SB("lattnT", [128, 2 * 128], BF16)
        mx = SB("mx", [128, 8], F32)
        den = SB("den", [128, 8], F32)
        rden = SB("rden", [128, 8], F32)
        gatebf = SB("gatebf", [2, 2 * COLS], BF16)
        stats = SB("stats", [1, 4 * COLS], F32)
        probs = [SB(f"probs{j}", [128, 4 * NE], F32) for j in range(2)]
        top2 = [SB(f"top2{j}", [128, 64], F32) for j in range(2)]
        cwtok = [SB(f"cwtok{j}", [128, 4 * NE], F32) for j in range(2)]
        cwtb = [SB(f"cwtb{j}", [128, 4 * NE], BF16) for j in range(2)]
        cwf = [SB(f"cwf{j}", [8, COLS], BF16) for j in range(2)]
        bias_bf = SB("bias_bf", [1, B_COLS], BF16)
        cstb = SB("cstb", [128, C_COLS], BF16)
        cstf_s = SB("cstf_s", [128, CF_COLS], F32)
        ones1b = SB("ones1b", [1, 512], BF16)
        ones128b = SB("ones128b", [128, 1], BF16)
        ones128f = SB("ones128f", [128, 1], F32)
        ones1f = SB("ones1f", [1, 128], F32)
        wrres = SB("wrres", [128, 64], BF16)

        pA = PS("pA", [128, 512], F32)
        pB = PS("pB", [128, 512], F32)
        pC = PS("pC", [128, 512], F32)
        pD = PS("pD", [128, 512], F32)
        pE = PS("pE", [128, 512], F32)
        pF = PS("pF", [128, 512], F32)
        pT = PS("pT", [128, 1024], BF16)
        pT2 = PS("pT2", [128, 1024], BF16)

        fw = FW(nc)
        bufs = {}

        def B(name):
            if name not in bufs:
                bufs[name] = fw.buf(name)
            return bufs[name]

        def BL(name, n):
            return [B(f"{name}.{j}") for j in range(n)]

        bY = [BL(f"y{j}", DC) for j in range(NYB)]
        blnA = BL("lnA", DC)
        blnB = BL("lnB", DC)
        blnF1 = BL("lnF1", DC)
        bh = BL("h", IC)
        bh2 = BL("h2", IC)
        bcwb = [B(f"cwb{j}") for j in range(4)]
        bsg = [B(f"sg{j}") for j in range(2)]
        bsqf = BL("sqf", 2)
        bscr = [B(f"scr{j}") for j in range(2)]
        bwr = [B(f"w{i}") for i in range(NWS)]
        bkb, bktail, bvb, bvtail = B("kb"), B("ktail"), B("vb"), B("vtail")
        brqb, bpooled, brk, brvf, brvlat = B("rqb"), B("pooled"), B("rkbuf"), B("rvfbuf"), B("rvlat")
        blatout = B("latout")
        blocal = bh2[0:DC]
        bremote = bh2[DC:]
        bpbf = BL("pbf", 4)
        battnT = BL("attnT", 4)
        blattnT = B("lattnT")
        bmx = BL("mx", 8)
        bden = BL("den", 8)
        brden = BL("rden", 8)
        bgate, bstats = B("gatebf"), B("stats")
        bprobs = [B(f"probs{j}") for j in range(2)]
        btop2 = [B(f"top2{j}") for j in range(2)]
        bcwtok = [B(f"cwtok{j}") for j in range(2)]
        bcwtb = [B(f"cwtb{j}") for j in range(2)]
        bcwf = [B(f"cwf{j}") for j in range(2)]
        bbias, bcst, bcstf = B("bias_bf"), B("cstb"), B("cstf_s")
        bones1b, bones128b, bones1f = B("ones1b"), B("ones128b"), B("ones1f")
        bones128f = B("ones128f")
        bwrres = B("wrres")
        bPA, bPB, bPC, bPD, bPE, bPF, bPT, bPT2 = (B(n) for n in
            ("pA", "pB", "pC", "pD", "pE", "pF", "pT", "pT2"))
        NWCH = (W_COLS + 4095) // 4096
        bwbf = BL("wbf", NWCH)
        bydram = B("ydram")

        fw.op("gpd", lambda g: g.dma_start(out=bias_bf[:], in_=bia_d), writes=[bbias], dma=True)
        fw.op("gpd", lambda g: g.dma_start(out=cstb[:], in_=cst_d), writes=[bcst], dma=True)
        fw.op("gpd", lambda g: g.dma_start(out=cstf_s[:], in_=cstf_d), writes=[bcstf], dma=True)
        fw.op("gpsimd", lambda g: g.memset(ones1b[:], 1.0), writes=[bones1b])
        fw.op("gpsimd", lambda g: g.memset(ones128b[:], 1.0), writes=[bones128b])
        fw.op("gpsimd", lambda g: g.memset(ones128f[:], 1.0), writes=[bones128f])
        fw.op("gpsimd", lambda g: g.memset(ones1f[:], 1.0), writes=[bones1f])
        fw.op("gpsimd", lambda g: g.memset(pooled[:], 0.0), writes=[bpooled])
        fw.op("gpsimd", lambda g: g.memset(rkbuf[:], 0.0), writes=[brk])
        fw.op("gpsimd", lambda g: g.memset(rvfbuf[:], 0.0), writes=[brvf])
        fw.op("gpsimd", lambda g: g.memset(rvlat[:], 0.0), writes=[brvlat])

        def load_x(i):
            yb, byb = y[i % NYB], bY[i % NYB]
            off = i * DC * COLS
            fw.op("gpd", lambda g, yb=yb, off=off: g.dma_start(out=yb[:], in_=x_d[:, off:off + DC * COLS]),
                  writes=byb, dma=True)

        woff = W_OFFS["wr"][0]
        fw.op("gpd", lambda g: g.dma_start(out=wrres[:], in_=wf_d[:, woff:woff + 64]),
              writes=[bwrres], dma=True)
        load_x(0)

        slot_state = {"i": 0, "f32": True}

        def stream(name, cols_a, cols_b):
            off, K, N = W_OFFS[name]
            si = slot_state["i"] % NWS
            slot_state["i"] += 1
            sl, bs = wring[si], bwr[si]
            a, b = off + cols_a, off + cols_b
            rng = list(range(a // 4096, (b - 1) // 4096 + 1))
            if slot_state["f32"]:
                fw.op("gpd", lambda g, sl=sl, a=a, b=b: g.dma_start(out=sl[:, 0:b - a], in_=wf_d[:, a:b]),
                      writes=[bs], dma=True)
                fw.op("sync", lambda s, sl=sl, a=a, b=b: s.dma_start(out=wbf_d[:, a:b], in_=sl[:, 0:b - a]),
                      reads=[bs], writes=[bwbf[j] for j in rng], dma=True)
                return sl, bs, cols_a
            fw.op("sync", lambda s, sl=sl, a=a, b=b: s.dma_start(out=sl[:, 0:b - a], in_=wbf_d[:, a:b]),
                  reads=[bwbf[j] for j in rng], writes=[bs], dma=True)
            return sl, bs, cols_a

        def CC(ch, s=0, p0=0):
            return ch * COLS + s * TP + p0

        def ln_p1(i, dst, bdst, ch):
            yb, byb = y[i % NYB], bY[i % NYB]
            fw.op("vector", lambda v, ch=ch: v.tensor_copy(dst[:, CC(ch):CC(ch) + COLS], yb[:, CC(ch):CC(ch) + COLS]),
                  reads=[byb[ch]], writes=[bdst[ch]])
            fw.op("tensor", lambda t, ch=ch: t.matmul(pA[0:1, 0:COLS], ones128b[:], dst[:, CC(ch):CC(ch) + COLS],
                                                      start=(ch == 0), stop=(ch == DC - 1), skip_group_check=True),
                  reads=[bones128b, bdst[ch]], writes=[bPA])
            sqs = (ch % 2) * COLS
            fw.op("scalar", lambda sx, ch=ch, sqs=sqs: sx.activation(sqf[:, sqs:sqs + COLS], yb[:, CC(ch):CC(ch) + COLS], AF.Square),
                  reads=[byb[ch]], writes=[bsqf[ch % 2]])
            fw.op("tensor", lambda t, ch=ch, sqs=sqs: t.matmul(pB[0:1, 0:COLS], ones128b[:], sqf[:, sqs:sqs + COLS],
                                                      start=(ch == 0), stop=(ch == DC - 1), skip_group_check=True),
                  reads=[bones128b, bsqf[ch % 2]], writes=[bPB])

        def spacer(tok):
            fw.op("tensor", lambda t: t.matmul(
                pB[0:1, 0:128], ones128b[0:1, 0:1], ones1b[0:1, 0:128],
                start=True, stop=True, skip_group_check=True),
                reads=[bones128b, bones1b], writes=[bPB], deps=[tok])

        def ln_p2(i, dst, bdst):
            yb, byb = y[i % NYB], bY[i % NYB]
            m_ = lambda: stats[0:1, 0:COLS]
            m2_ = lambda: stats[0:1, COLS:2 * COLS]
            sqm_ = lambda: stats[0:1, 2 * COLS:3 * COLS]
            rstd_ = lambda: stats[0:1, 3 * COLS:4 * COLS]
            mrs_ = m2_
            var_ = m2_
            fw.op("scalar", lambda sx: sx.activation(m_(), pA[0:1, 0:COLS], AF.Copy, scale=1.0 / D),
                  reads=[bPA], writes=[bstats])
            tk = fw.op("scalar", lambda sx: sx.activation(sqm_(), pB[0:1, 0:COLS], AF.Copy, scale=1.0 / D),
                  reads=[bPB], writes=[bstats])
            spacer(tk)
            tk = fw.op("scalar", lambda sx: sx.activation(m2_(), m_(), AF.Square),
                  reads=[bstats], writes=[bstats])
            spacer(tk)
            tk = fw.op("vector", lambda v: v.tensor_sub(var_(), sqm_(), m2_()),
                  reads=[bstats], writes=[bstats])
            spacer(tk)
            tk = fw.op("scalar", lambda sx: sx.activation(sqm_(), var_(), AF.Ln, bias=cstf_s[0:1, 4:5]),
                  reads=[bstats, bcstf], writes=[bstats])
            spacer(tk)
            tk = fw.op("scalar", lambda sx: sx.activation(rstd_(), sqm_(), AF.Exp, scale=-0.5),
                  reads=[bstats], writes=[bstats])
            spacer(tk)
            fw.op("vector", lambda v: v.tensor_mul(mrs_(), m_(), rstd_()),
                  reads=[bstats], writes=[bstats])
            fw.op("tensor", lambda t: t.matmul(pE[:, 0:COLS], ones1f[0:1, 0:128], rstd_(), start=True, stop=True),
                  reads=[bones1f, bstats], writes=[bPE])
            fw.op("tensor", lambda t: t.matmul(pF[:, 0:COLS], ones1f[0:1, 0:128], mrs_(), start=True, stop=True),
                  reads=[bones1f, bstats], writes=[bPF])
            for ch in range(DC):
                sc, bsc = scr[ch % 2], bscr[ch % 2]
                fw.op("vector", lambda v, ch=ch, sc=sc: v.tensor_mul(sc[:], yb[:, CC(ch):CC(ch) + COLS], pE[:, 0:COLS]),
                      reads=[byb[ch], bPE], writes=[bsc])
                tk = fw.op("vector", lambda v, ch=ch, sc=sc: v.tensor_sub(dst[:, CC(ch):CC(ch) + COLS], sc[:], pF[:, 0:COLS]),
                      reads=[bsc, bPF], writes=[bdst[ch]])
                if ch % 2 == 1:
                    spacer(tk)

        def ln_pass(i, dst, bdst):
            for ch in range(DC):
                ln_p1(i, dst, bdst, ch)
            ln_p2(i, dst, bdst)

        def routing(src, bsrc, rj):
            prb, tpb, ctk, ctb, cwfb = probs[rj], top2[rj], cwtok[rj], cwtb[rj], cwf[rj]
            bprb, btpb, bctk, bctb, bcwfb = bprobs[rj], btop2[rj], bcwtok[rj], bcwtb[rj], bcwf[rj]
            for kc in range(DC):
                for tc in range(4):
                    fw.op("tensor", lambda t, tc=tc, kc=kc: t.matmul(
                        pD[0:128, tc * 8:tc * 8 + 8],
                        src[:, kc * COLS + tc * 128: kc * COLS + tc * 128 + 128],
                        wrres[:, kc * 8:kc * 8 + 8],
                        start=(kc == 0 and tc == 0), stop=False, skip_group_check=True),
                        reads=[bsrc[kc], bwrres], writes=[bPD], signal=False)
            for tc in range(4):
                fw.op("tensor", lambda t, tc=tc: t.matmul(
                    pD[0:128, tc * 8:tc * 8 + 8], ones1b[0:1, 0:128],
                    bias_bf[0:1, B_OFFS["rb"]:B_OFFS["rb"] + 8],
                    start=False, stop=(tc == 3), skip_group_check=True),
                    reads=[bones1b, bbias], writes=[bPD], signal=(tc == 3))
            fw.op("scalar", lambda s: s.activation(prb[:], pD[0:128, 0:32], AF.Sigmoid),
                  reads=[bPD], writes=[bprb])
            for tc in range(4):
                pr = prb[:, tc * 8:(tc + 1) * 8]
                m1 = tpb[:, tc * 16:tc * 16 + 1]
                ge1 = tpb[:, tc * 16 + 2:tc * 16 + 10]
                cnt = tpb[:, tc * 16 + 10:tc * 16 + 11]
                strict = tpb[:, tc * 16 + 11:tc * 16 + 12]
                cnt2 = tpb[:, tc * 16 + 12:tc * 16 + 13]
                dd = tpb[:, tc * 16 + 13:tc * 16 + 14]
                th = tpb[:, tc * 16 + 14:tc * 16 + 15]
                fw.op("vector", lambda v, pr=pr, m1=m1: v.reduce_max(m1, pr, axis=AX.X), reads=[bprb], writes=[btpb])
                fw.op("vector", lambda v, pr=pr, ge1=ge1, m1=m1: v.tensor_scalar(ge1, pr, m1, None, ALU.is_ge),
                      reads=[bprb, btpb], writes=[btpb])
                fw.op("vector", lambda v, ge1=ge1, cnt=cnt: v.reduce_sum(cnt, ge1, axis=AX.X), reads=[btpb], writes=[btpb])
                fw.op("vector", lambda v, pr=pr, ge1=ge1, tc=tc: v.scalar_tensor_tensor(
                    ctk[:, tc * 8:(tc + 1) * 8], ge1, -1e30, pr, ALU.mult, ALU.add),
                    reads=[bprb, btpb], writes=[bctk])
                fw.op("vector", lambda v, tc=tc, strict=strict: v.reduce_max(strict, ctk[:, tc * 8:(tc + 1) * 8], axis=AX.X),
                      reads=[bctk], writes=[btpb])
                fw.op("vector", lambda v, cnt=cnt, cnt2=cnt2: v.tensor_scalar(cnt2, cnt, 2.0, None, ALU.is_ge),
                      reads=[btpb], writes=[btpb])
                fw.op("vector", lambda v, m1=m1, strict=strict, dd=dd: v.tensor_sub(dd, m1, strict),
                      reads=[btpb], writes=[btpb])
                fw.op("vector", lambda v, cnt2=cnt2, dd=dd, th=th: v.scalar_tensor_tensor(
                    th, cnt2, 1.0, dd, ALU.mult, ALU.mult), reads=[btpb], writes=[btpb])
                fw.op("vector", lambda v, th=th, strict=strict: v.tensor_add(th, th, strict),
                      reads=[btpb], writes=[btpb])
                fw.op("vector", lambda v, pr=pr, th=th, ge1=ge1: v.tensor_scalar(ge1, pr, th, None, ALU.is_ge),
                      reads=[bprb, btpb], writes=[btpb])
                fw.op("vector", lambda v, pr=pr, ge1=ge1, tc=tc: v.tensor_mul(ctk[:, tc * 8:(tc + 1) * 8], pr, ge1),
                      reads=[bprb, btpb], writes=[bctk])
            fw.op("vector", lambda v: v.tensor_copy(ctb[:], ctk[:]), reads=[bctk], writes=[bctb])

            def emit_cwf():
                for tc in range(4):
                    fw.op("tensor", lambda t, tc=tc: t.transpose(
                        pT[0:8, tc * 128:tc * 128 + 128], ctb[:, tc * 8:(tc + 1) * 8], cstb[:, C_ID:C_ID + 128]),
                        reads=[bctb, bcst], writes=[bPT])
                for tc in range(4):
                    fw.op("vector", lambda v, tc=tc: v.tensor_copy(
                        cwfb[:, tc * 128:(tc + 1) * 128], pT[0:8, tc * 128:tc * 128 + 128]),
                        reads=[bPT], writes=[bcwfb])

            def emit_bc(e):
                sl_ = 2 * rj + e % 2
                pp, bpp = (pE, bPE) if e % 2 == 0 else (pF, bPF)
                fw.op("tensor", lambda t, e=e, pp=pp: t.matmul(
                    pp[:, 0:COLS], cstb[0:8, C_SELE + e * 128:C_SELE + (e + 1) * 128],
                    cwfb[:], start=True, stop=True),
                    reads=[bcst, bcwfb], writes=[bpp])
                fw.op("scalar", lambda s, sl_=sl_, pp=pp: s.activation(
                    cwb[:, sl_ * COLS:(sl_ + 1) * COLS], pp[:, 0:COLS], AF.Copy),
                    reads=[bpp], writes=[bcwb[sl_]])
            return emit_cwf, emit_bc

        def ffn_multi(passes):
            emits = []
            for p in passes:
                if p.get("pre") is not None:
                    p["pre"]()
                emits.append(routing(p["src"], p["bsrc"], p["rj"]))
            experts = [("gs", "us", "ds", None)] + [(f"g{e}", f"u{e}", f"d{e}", e) for e in range(NE)]
            np_ = len(passes)
            for (gn, un, dn, e) in experts:
                grp = 0
                for sc4 in range(4):
                    sl_g, bs_g, _ = stream(gn, sc4 * 4096, (sc4 + 1) * 4096)
                    sl_u, bs_u, _ = stream(un, sc4 * 4096, (sc4 + 1) * 4096)
                    for mcl in range(4):
                        mc = sc4 * 4 + mcl
                        for pi, p in enumerate(passes):
                            src, bsrc = p["src"], p["bsrc"]
                            hbuf, bhbuf = p["h"], p["bh"]
                            pg, bpg = (pA, bPA) if grp % 2 == 0 else (pC, bPC)
                            pu, bpu = (pB, bPB) if grp % 2 == 0 else (pD, bPD)
                            for kc in range(DC):
                                if kc == DC - 1:
                                    fw.prewait("tensor", reads=[bs_u], writes=[bpu])
                                fw.op("tensor", lambda t, sl=sl_g, mcl=mcl, kc=kc, pg=pg, src=src: t.matmul(
                                    pg[:, 0:COLS],
                                    sl[:, mcl * D + kc * 128: mcl * D + kc * 128 + 128],
                                    src[:, kc * COLS:(kc + 1) * COLS],
                                    start=(kc == 0), stop=(kc == DC - 1)),
                                    reads=[bs_g, bsrc[kc]], writes=[bpg], signal=(kc == DC - 1))
                            sgb, bsgb = sg[grp % 2], bsg[grp % 2]
                            fw.op("scalar", lambda s, sgb=sgb, pg=pg: s.activation(sgb[:], pg[:, 0:COLS], AF.Silu),
                                  reads=[bpg], writes=[bsgb])
                            last_gu = (mcl == 3 and pi == np_ - 1)
                            for kc in range(DC):
                                if kc == DC - 1 and not last_gu:
                                    pgn = bPA if (grp + 1) % 2 == 0 else bPC
                                    fw.prewait("tensor", writes=[pgn])
                                fw.op("tensor", lambda t, sl=sl_u, mcl=mcl, kc=kc, pu=pu, src=src: t.matmul(
                                    pu[:, 0:COLS],
                                    sl[:, mcl * D + kc * 128: mcl * D + kc * 128 + 128],
                                    src[:, kc * COLS:(kc + 1) * COLS],
                                    start=(kc == 0), stop=(kc == DC - 1)),
                                    reads=[bs_u, bsrc[kc]], writes=[bpu], signal=(kc == DC - 1))
                            if e is None:
                                fw.op("vector", lambda v, mc=mc, sgb=sgb, pu=pu, hbuf=hbuf: v.tensor_mul(
                                    hbuf[:, mc * COLS:(mc + 1) * COLS], sgb[:], pu[:, 0:COLS]),
                                    reads=[bsgb, bpu], writes=[bhbuf[mc]])
                            else:
                                sc2, bsc2 = scr[grp % 2], bscr[grp % 2]
                                cslot = 2 * p["rj"] + (e % 2)
                                fw.op("vector", lambda v, sgb=sgb, sc2=sc2, pu=pu: v.tensor_mul(sc2[:], sgb[:], pu[:, 0:COLS]),
                                      reads=[bsgb, bpu], writes=[bsc2])
                                fw.op("vector", lambda v, mc=mc, sc2=sc2, hbuf=hbuf, cslot=cslot: v.tensor_mul(
                                    hbuf[:, mc * COLS:(mc + 1) * COLS], sc2[:],
                                    cwb[:, cslot * COLS:(cslot + 1) * COLS]),
                                    reads=[bsc2, bcwb[cslot]], writes=[bhbuf[mc]])
                            grp += 1
                if e is None:
                    for (emit_cwf, emit_bc) in emits:
                        emit_cwf()
                        emit_bc(0)
                        emit_bc(1)
                elif e + 2 < NE:
                    for (emit_cwf, emit_bc) in emits:
                        emit_bc(e + 2)
                dgrp = 0
                for sc2_ in range(4):
                    sl, bs, _ = stream(dn, sc2_ * 4096, (sc2_ + 1) * 4096)
                    for mcl in range(2):
                        oc = sc2_ * 2 + mcl
                        for pi, p in enumerate(passes):
                            hbuf, bhbuf = p["h"], p["bh"]
                            pp, bpp = (pE, bPE) if dgrp % 2 == 0 else (pF, bPF)
                            last_d = (oc == 7 and pi == np_ - 1)
                            for kc in range(IC):
                                if kc == IC - 1 and not last_d:
                                    ppn = bPE if (dgrp + 1) % 2 == 0 else bPF
                                    fw.prewait("tensor", writes=[ppn])
                                fw.op("tensor", lambda t, sl=sl, mcl=mcl, kc=kc, pp=pp, hbuf=hbuf: t.matmul(
                                    pp[:, 0:COLS],
                                    sl[:, mcl * I + kc * 128: mcl * I + kc * 128 + 128],
                                    hbuf[:, kc * COLS:(kc + 1) * COLS],
                                    start=(kc == 0), stop=(kc == IC - 1)),
                                    reads=[bs, bhbuf[kc]], writes=[bpp], signal=(kc == IC - 1))
                            yb, byb = p["yb"], p["byb"]
                            fw.op("vector", lambda v, oc=oc, pp=pp, yb=yb: v.scalar_tensor_tensor(
                                yb[:, oc * COLS:(oc + 1) * COLS], pp[:, 0:COLS], 0.5,
                                yb[:, oc * COLS:(oc + 1) * COLS], ALU.mult, ALU.add),
                                reads=[bpp, byb[oc]], writes=[byb[oc]])
                            if e == NE - 1 and p.get("post_chunk") is not None:
                                p["post_chunk"](oc)
                            dgrp += 1

        def fm_proj(name, src, bsrc, dst, bdst, mc_range, bias_off, scale=1.0):
            off, K, N = W_OFFS[name]
            KC = K // 128
            mc0 = mc_range[0]
            per_slot = 4096 // K
            idx = 0
            while idx < len(mc_range):
                group = mc_range[idx:idx + per_slot]
                sl, bs, _ = stream(name, group[0] * K, (group[-1] + 1) * K)
                for gi, mc in enumerate(group):
                    pp, bpp = (pA, bPA) if (mc % 2 == 0) else (pB, bPB)
                    for kc in range(KC):
                        fw.op("tensor", lambda t, sl=sl, gi=gi, kc=kc, pp=pp, K=K: t.matmul(
                            pp[:, 0:COLS],
                            sl[:, gi * K + kc * 128: gi * K + kc * 128 + 128],
                            src[:, kc * COLS:(kc + 1) * COLS],
                            start=(kc == 0), stop=False),
                            reads=[bs, bsrc[kc]], writes=[bpp], signal=False)
                    fw.op("tensor", lambda t, mc=mc, pp=pp: t.matmul(
                        pp[:, 0:COLS], bias_bf[0:1, bias_off + mc * 128: bias_off + mc * 128 + 128],
                        ones1b[0:1, 0:COLS], start=False, stop=True),
                        reads=[bbias, bones1b], writes=[bpp])
                    dc_ = mc - mc0
                    fw.op("scalar", lambda s, dc_=dc_, pp=pp, scale=scale: s.activation(
                        dst[:, dc_ * COLS:(dc_ + 1) * COLS], pp[:, 0:COLS], AF.Copy, scale=scale),
                        reads=[bpp], writes=[bdst if not isinstance(bdst, list) else bdst[dc_]])
                idx += per_slot

        def attn(i, pre=None, post_chunk=None, mid=None, mid2=None):
            src, bsrc = lnB, blnB
            if pre is None:
                ln_pass(i, src, bsrc)
            else:
                pre()
            fm_proj("qkv", src, bsrc, qb, bqb_l, list(range(0, 8)), B_OFFS["qkv"], scale=0.125)
            fm_proj("qkv", src, bsrc, kb, bkb_l, list(range(8, 16)), B_OFFS["qkv"])
            for half in range(2):
                sl, bs, _ = stream("qkv", (16 + 4 * half) * D, (20 + 4 * half) * D)
                for tc in range(4):
                    pp, bpp = (pA, bPA) if tc % 2 == 0 else (pB, bPB)
                    for kc in range(DC):
                        fw.op("tensor", lambda t, sl=sl, tc=tc, kc=kc, pp=pp: t.matmul(
                            pp[:, 0:512],
                            src[:, kc * COLS + tc * 128: kc * COLS + tc * 128 + 128],
                            sl.rearrange("p (mc k) -> p mc k", k=D)[:, :, kc * 128:kc * 128 + 128],
                            start=(kc == 0), stop=False),
                            reads=[bsrc[kc], bs], writes=[bpp], signal=False)
                    fw.op("tensor", lambda t, half=half, tc=tc, pp=pp: t.matmul(
                        pp[:, 0:512], ones1b[0:1, 0:128],
                        bias_bf[0:1, B_OFFS["qkv"] + 2048 + half * 512: B_OFFS["qkv"] + 2048 + half * 512 + 512],
                        start=False, stop=True),
                        reads=[bones1b, bbias], writes=[bpp])
                    fw.op("scalar", lambda s, half=half, tc=tc, pp=pp: s.activation(
                        vb[:, tc * D + half * 512: tc * D + half * 512 + 512], pp[:, 0:512], AF.Copy),
                        reads=[bpp], writes=[bvb])
            fm_proj("rq", src, bsrc, rqb, brqb, [0, 1], B_OFFS["rq"], scale=0.125)
            for ch in range(DC):
                for s_ in range(SEQ):
                    dst = pooled[:, ch * 128 + s_ * 64 + 4 * i: ch * 128 + s_ * 64 + 4 * i + 4]
                    srcv = src.ap().rearrange("p (c8 s2 a b) -> p c8 s2 a b", c8=DC, s2=SEQ, a=4)[:, ch, s_]
                    sc2, bsc2 = scr[ch % 2], bscr[ch % 2]
                    fw.op("vector", lambda v, srcv=srcv, sc2=sc2: v.reduce_sum(
                        sc2.ap().rearrange("p (a b) -> p a b", a=4)[:, :, 0:1], srcv, axis=AX.X),
                        reads=[bsrc[ch]], writes=[bsc2])
                    fw.op("vector", lambda v, dst=dst, sc2=sc2: v.tensor_copy(dst, sc2[:, 0:4]),
                        reads=[bsc2], writes=[bpooled])
            sl, bs, _ = stream("rkv", 0, 4096)
            for mc in range(4):
                pp, bpp = (pA, bPA) if mc % 2 == 0 else (pB, bPB)
                for kc in range(DC):
                    fw.op("tensor", lambda t, sl=sl, mc=mc, kc=kc, pp=pp: t.matmul(
                        pp[:, 0:128],
                        sl[:, mc * D + kc * 128: mc * D + kc * 128 + 128],
                        pooled[:, kc * 128:(kc + 1) * 128],
                        start=(kc == 0), stop=(kc == DC - 1)),
                        reads=[bs, bpooled], writes=[bpp], signal=(kc == DC - 1))
                dstb, bdstb = (rkbuf, brk) if mc < 2 else (rvfbuf, brvf)
                fw.op("vector", lambda v, mc=mc, pp=pp, dstb=dstb: v.tensor_scalar(
                    dstb[:, (mc % 2) * 128:(mc % 2) * 128 + 128], pp[:, 0:128],
                    1.0 / 64, cstf_s[:, mc:mc + 1], ALU.mult, ALU.add),
                    reads=[bpp, bcstf], writes=[bdstb])
            for s_ in range(SEQ):
                for hh in range(4):
                    fw.op("tensor", lambda t, s_=s_, hh=hh, hb=64 * (hh % 2): t.transpose(
                        pT[0:64, 0:64],
                        rvfbuf[hb:hb + 64, (hh // 2) * 128 + s_ * 64: (hh // 2) * 128 + s_ * 64 + 64],
                        cstb[hb:hb + 64, C_ID + hb:C_ID + hb + 64]),
                        reads=[brvf, bcst], writes=[bPT])
                    fw.op("vector", lambda v, s_=s_, hh=hh: v.tensor_copy(
                        rvlat[64 * s_:64 * s_ + 64, hh * 64:hh * 64 + 64], pT[0:64, 0:64]),
                        reads=[bPT], writes=[brvlat])
            sc_banks = [(pA, bPA), (pB, bPB), (pE, bPE), (pF, bPF)]
            tr_banks = [(pT, bPT), (pT2, bPT2)]
            out_banks = [(pC, bPC), (pD, bPD)]
            groups = [(s_, qc, hh) for s_ in range(SEQ) for qc in range(2) for hh in range(NH)]
            stages = []
            for gidx, (s_, qc, hh) in enumerate(groups):
                first = (i == 0 and qc == 0)
                rot = gidx % 4
                psc, bpsc = sc_banks[rot]
                ptr, bptr = tr_banks[gidx % 2]
                pout, bpout = out_banks[gidx % 2]
                pbo = pbf[:, rot * 256: rot * 256 + 256]
                ato = attnT[:, rot * 256: rot * 256 + 256]
                bpbo, bato = bpbf[rot], battnT[rot]
                mxo, deno, rdeno = mx[:, rot:rot + 1], den[:, rot:rot + 1], rden[:, rot:rot + 1]
                bmxo, bdeno, brdeno = bmx[rot], bden[rot], brden[rot]
                hch, hbase = hh // 2, 64 * (hh % 2)
                qa = qb[hbase:hbase + 64, hch * COLS + s_ * TP + qc * 128: hch * COLS + s_ * TP + qc * 128 + 128]
                W_ = 128 if first else 256
                nkc = 1 if first else 2
                trc = (rot // 2) * 512

                def S0(first=first, qa=qa, psc=psc, bpsc=bpsc, hch=hch, hbase=hbase, s_=s_, qc=qc):
                    if first:
                        fw.op("tensor", lambda t: t.matmul(
                            psc[:, 0:128], qa,
                            kb[hbase:hbase + 64, hch * COLS + s_ * TP: hch * COLS + s_ * TP + 128],
                            start=True, stop=False, skip_group_check=True),
                            reads=[blnF1[hch], bkb], writes=[bpsc], signal=False)
                        fw.op("tensor", lambda t: t.matmul(
                            psc[:, 0:128], cstb[:, C_ID:C_ID + 128], cstb[:, C_M128:C_M128 + 128],
                            start=False, stop=True, skip_group_check=True),
                            reads=[bcst], writes=[bpsc])
                    else:
                        ksrc0 = (ktail[hbase:hbase + 64, hch * SEQ * 128 + s_ * 128: hch * SEQ * 128 + s_ * 128 + 128]
                                 if qc == 0 else
                                 kb[hbase:hbase + 64, hch * COLS + s_ * TP: hch * COLS + s_ * TP + 128])
                        bks0 = bktail if qc == 0 else bkb
                        ksrc1 = kb[hbase:hbase + 64, hch * COLS + s_ * TP + qc * 128: hch * COLS + s_ * TP + qc * 128 + 128]
                        fw.op("tensor", lambda t: t.matmul(
                            psc[:, 0:128], qa, ksrc0,
                            start=True, stop=False, skip_group_check=True),
                            reads=[blnF1[hch], bks0], writes=[bpsc], signal=False)
                        fw.op("tensor", lambda t: t.matmul(
                            psc[:, 128:256], qa, ksrc1,
                            start=False, stop=False, skip_group_check=True),
                            reads=[blnF1[hch], bkb], writes=[bpsc], signal=False)
                        fw.op("tensor", lambda t: t.matmul(
                            psc[:, 0:256], cstb[:, C_ID:C_ID + 128], cstb[:, C_M256:C_M256 + 256],
                            start=False, stop=True, skip_group_check=True),
                            reads=[bcst], writes=[bpsc])

                def S1(psc=psc, bpsc=bpsc, mxo=mxo, bmxo=bmxo, W_=W_):
                    fw.op("vector", lambda v: v.reduce_max(mxo, psc[:, 0:W_], axis=AX.X, negate=True),
                          reads=[bpsc], writes=[bmxo])

                def S2(psc=psc, bpsc=bpsc, pbo=pbo, bpbo=bpbo, mxo=mxo, bmxo=bmxo, deno=deno, bdeno=bdeno, W_=W_):
                    fw.op("scalar", lambda sx: sx.activation(pbo[:, 0:W_], psc[:, 0:W_], AF.Exp, bias=mxo, accum_out=deno),
                          reads=[bpsc, bmxo], writes=[bpbo, bdeno])

                def S3(pbo=pbo, bpbo=bpbo, deno=deno, bdeno=bdeno, rdeno=rdeno, brdeno=brdeno, W_=W_):
                    fw.op("vector", lambda v: v.reciprocal(rdeno, deno), reads=[bdeno], writes=[brdeno])
                    fw.op("vector", lambda v: v.tensor_scalar(pbo[:, 0:W_], pbo[:, 0:W_], rdeno, None, ALU.mult),
                          reads=[brdeno], writes=[bpbo])

                def S4(pbo=pbo, bpbo=bpbo, ptr=ptr, bptr=bptr, trc=trc, nkc=nkc):
                    for kcb in range(nkc):
                        fw.op("tensor", lambda t, kcb=kcb: t.transpose(
                            ptr[:, trc + kcb * 128: trc + kcb * 128 + 128],
                            pbo[:, kcb * 128:kcb * 128 + 128],
                            cstb[:, C_ID:C_ID + 128]),
                            reads=[bpbo, bcst], writes=[bptr])

                def S5(ato=ato, bato=bato, ptr=ptr, bptr=bptr, trc=trc, nkc=nkc):
                    for kcb in range(nkc):
                        if kcb == 0:
                            fw.op("vector", lambda v, kcb=kcb: v.tensor_copy(
                                ato[:, kcb * 128:kcb * 128 + 128],
                                ptr[:, trc + kcb * 128: trc + kcb * 128 + 128]),
                                reads=[bptr], writes=[bato])
                        else:
                            fw.op("scalar", lambda sx, kcb=kcb: sx.activation(
                                ato[:, kcb * 128:kcb * 128 + 128],
                                ptr[:, trc + kcb * 128: trc + kcb * 128 + 128], AF.Copy),
                                reads=[bptr], writes=[bato])

                def S6(first=first, s_=s_, qc=qc, hh=hh, ato=ato, bato=bato, pout=pout, bpout=bpout, nkc=nkc):
                    for kcb in range(nkc):
                        if first:
                            vsrc, bvsrc = vb[:, (s_ * 2) * D + hh * 64:(s_ * 2) * D + hh * 64 + 64], bvb
                        elif qc == 0 and kcb == 0:
                            vsrc, bvsrc = vtail[:, s_ * D + hh * 64: s_ * D + hh * 64 + 64], bvtail
                        else:
                            tc = s_ * 2 + (qc - 1 + kcb)
                            vsrc, bvsrc = vb[:, tc * D + hh * 64: tc * D + hh * 64 + 64], bvb
                        fw.op("tensor", lambda t, vsrc=vsrc, bvsrc=bvsrc, kcb=kcb: t.matmul(
                            pout[0:64, 0:128], vsrc, ato[:, kcb * 128:kcb * 128 + 128],
                            start=(kcb == 0), stop=(kcb == nkc - 1)),
                            reads=[bvsrc, bato], writes=[bpout], signal=(kcb == nkc - 1))

                def S7(hch=hch, hbase=hbase, s_=s_, qc=qc, pout=pout, bpout=bpout):
                    fw.op("scalar", lambda sx: sx.activation(
                        h2[hbase:hbase + 64, hch * COLS + s_ * TP + qc * 128: hch * COLS + s_ * TP + qc * 128 + 128],
                        pout[0:64, 0:128], AF.Copy),
                        reads=[bpout], writes=[blocal[hch]])

                stages.append([S0, S1, S2, S3, S4, S5, S6, S7])
            NS = 8
            for step in range(len(stages) + NS - 1):
                for st in range(NS - 1, -1, -1):
                    g = step - st
                    if 0 <= g < len(stages):
                        stages[g][st]()
            if i == 0:
                for s_ in range(SEQ):
                    for ch in range(DC):
                        fw.op("vector", lambda v, s_=s_, ch=ch: v.memset(
                            h2[:, ch * COLS + s_ * TP: ch * COLS + s_ * TP + 1], 0.0),
                            writes=[blocal[ch]])
            lgroups = [(s_, qc, hh) for s_ in range(SEQ) for qc in range(2) for hh in range(4)]
            lstages = []
            for gidx, (s_, qc, hh) in enumerate(lgroups):
                Cq = 4 * i + 2 * qc
                rot = gidx % 4
                psc, bpsc = sc_banks[rot]
                ptr, bptr = tr_banks[gidx % 2]
                pout, bpout = out_banks[gidx % 2]
                pbo = pbf[:, rot * 256: rot * 256 + 64]
                bpbo = bpbf[rot]
                mxo, deno, rdeno = mx[:, 4 + rot:5 + rot], den[:, 4 + rot:5 + rot], rden[:, 4 + rot:5 + rot]
                bmxo, bdeno, brdeno = bmx[4 + rot], bden[4 + rot], brden[4 + rot]
                hch, hbase = hh // 2, 64 * (hh % 2)
                rqa = rqb[hbase:hbase + 64, hch * COLS + s_ * TP + qc * 128: hch * COLS + s_ * TP + qc * 128 + 128]
                trc = (rot // 2) * 512 + 256
                lslot = (rot % 2) * 128

                def L0(rqa=rqa, psc=psc, bpsc=bpsc, hch=hch, hbase=hbase, s_=s_, Cq=Cq):
                    fw.op("tensor", lambda t: t.matmul(
                        psc[:, 0:64], rqa,
                        rkbuf[hbase:hbase + 64, hch * 128 + s_ * 64: hch * 128 + s_ * 64 + 64],
                        start=True, stop=False, skip_group_check=True),
                        reads=[brqb, brk], writes=[bpsc], signal=False)
                    fw.op("tensor", lambda t: t.matmul(
                        psc[:, 0:64], cstb[:, C_ID:C_ID + 128],
                        cstb[:, C_LMASK + 64 - Cq: C_LMASK + 128 - Cq],
                        start=False, stop=True, skip_group_check=True),
                        reads=[bcst], writes=[bpsc])

                def L1(psc=psc, bpsc=bpsc, mxo=mxo, bmxo=bmxo):
                    fw.op("vector", lambda v: v.reduce_max(mxo, psc[:, 0:64], axis=AX.X, negate=True),
                          reads=[bpsc], writes=[bmxo])

                def L2(psc=psc, bpsc=bpsc, pbo=pbo, bpbo=bpbo, mxo=mxo, bmxo=bmxo, deno=deno, bdeno=bdeno):
                    fw.op("scalar", lambda sx: sx.activation(pbo, psc[:, 0:64], AF.Exp, bias=mxo, accum_out=deno),
                          reads=[bpsc, bmxo], writes=[bpbo, bdeno])

                def L3(pbo=pbo, bpbo=bpbo, deno=deno, bdeno=bdeno, rdeno=rdeno, brdeno=brdeno, qc=qc):
                    fw.op("vector", lambda v: v.reciprocal(rdeno, deno), reads=[bdeno], writes=[brdeno])
                    fw.op("vector", lambda v: v.tensor_scalar(pbo, pbo, rdeno, None, ALU.mult),
                          reads=[brdeno], writes=[bpbo])
                    if i == 0 and qc == 0:
                        fw.op("vector", lambda v: v.memset(pbo[0:64, :], 0.0), writes=[bpbo])

                def L4(pbo=pbo, bpbo=bpbo, ptr=ptr, bptr=bptr, trc=trc):
                    fw.op("tensor", lambda t: t.transpose(
                        ptr[0:64, trc: trc + 128], pbo, cstb[:, C_ID:C_ID + 128]),
                        reads=[bpbo, bcst], writes=[bptr])

                def L5(s_=s_, ptr=ptr, bptr=bptr, trc=trc, lslot=lslot):
                    fw.op("vector", lambda v: v.tensor_copy(
                        lattnT[64 * s_:64 * s_ + 64, lslot:lslot + 128],
                        ptr[0:64, trc: trc + 128]),
                        reads=[bptr], writes=[blattnT])

                def L6(s_=s_, hh=hh, lslot=lslot, pout=pout, bpout=bpout):
                    fw.op("tensor", lambda t: t.matmul(
                        pout[0:64, 128:256],
                        rvlat[64 * s_:64 * s_ + 64, hh * 64:hh * 64 + 64],
                        lattnT[64 * s_:64 * s_ + 64, lslot:lslot + 128],
                        start=True, stop=True, skip_group_check=True),
                        reads=[brvlat, blattnT], writes=[bpout])

                def L7(hch=hch, hbase=hbase, s_=s_, qc=qc, pout=pout, bpout=bpout):
                    fw.op("scalar", lambda sx: sx.activation(
                        latout[hbase:hbase + 64, hch * COLS + s_ * TP + qc * 128: hch * COLS + s_ * TP + qc * 128 + 128],
                        pout[0:64, 128:256], AF.Copy),
                        reads=[bpout], writes=[blatout])

                lstages.append([L0, L1, L2, L3, L4, L5, L6, L7])
            for step in range(len(lstages) + NS - 1):
                for st in range(NS - 1, -1, -1):
                    g = step - st
                    if 0 <= g < len(lstages):
                        lstages[g][st]()
            sl, bs, _ = stream("rout", 0, 2048)
            for mc in range(DC):
                pp, bpp = (pA, bPA) if mc % 2 == 0 else (pB, bPB)
                for kc in range(2):
                    fw.op("tensor", lambda t, sl=sl, mc=mc, kc=kc, pp=pp: t.matmul(
                        pp[:, 0:COLS], sl[:, mc * LD + kc * 128: mc * LD + kc * 128 + 128],
                        latout[:, kc * COLS:(kc + 1) * COLS],
                        start=(kc == 0), stop=False),
                        reads=[bs, blatout], writes=[bpp], signal=False)
                fw.op("tensor", lambda t, mc=mc, pp=pp: t.matmul(
                    pp[:, 0:COLS], bias_bf[0:1, B_OFFS["rout"] + mc * 128: B_OFFS["rout"] + mc * 128 + 128],
                    ones1b[0:1, 0:COLS], start=False, stop=True),
                    reads=[bbias, bones1b], writes=[bpp])
                fw.op("scalar", lambda s, mc=mc, pp=pp: s.activation(
                    h2[:, (DC + mc) * COLS:(DC + mc + 1) * COLS], pp[:, 0:COLS], AF.Copy),
                    reads=[bpp], writes=[bremote[mc]])
            if mid is not None:
                mid()
            gstages = []
            for ch in range(DC):
                pg_, bpg_ = (pE, bPE) if ch % 2 == 0 else (pF, bPF)
                pb_, bpb_ = (pC, bPC) if ch % 2 == 0 else (pD, bPD)
                gsl = ch % 2
                sc2, bsc2 = scr[ch % 2], bscr[ch % 2]

                def G0(ch=ch, pg_=pg_, bpg_=bpg_):
                    fw.op("tensor", lambda t: t.matmul(
                        pg_[0:2, 0:COLS], cstb[:, C_GL:C_GL + 2], h2[:, ch * COLS:(ch + 1) * COLS],
                        start=True, stop=False),
                        reads=[bcst, blocal[ch]], writes=[bpg_], signal=False)
                    fw.op("tensor", lambda t: t.matmul(
                        pg_[0:2, 0:COLS], cstb[:, C_GR:C_GR + 2], h2[:, (DC + ch) * COLS:(DC + ch + 1) * COLS],
                        start=False, stop=True),
                        reads=[bcst, bremote[ch]], writes=[bpg_])

                def G1(pg_=pg_, bpg_=bpg_, gsl=gsl):
                    fw.op("scalar", lambda sx: sx.activation(gatebf[:, gsl * COLS:(gsl + 1) * COLS], pg_[0:2, 0:COLS], AF.Sigmoid),
                          reads=[bpg_], writes=[bgate])

                def G2(pb_=pb_, bpb_=bpb_, gsl=gsl):
                    fw.op("tensor", lambda t: t.matmul(
                        pb_[:, 0:COLS], cstb[0:2, C_SEL2:C_SEL2 + 128], gatebf[:, gsl * COLS:(gsl + 1) * COLS],
                        start=True, stop=True),
                        reads=[bcst, bgate], writes=[bpb_])

                def G3(ch=ch, sc2=sc2, bsc2=bsc2):
                    fw.op("vector", lambda v: v.tensor_sub(
                        sc2[:], h2[:, ch * COLS:(ch + 1) * COLS], h2[:, (DC + ch) * COLS:(DC + ch + 1) * COLS]),
                        reads=[blocal[ch], bremote[ch]], writes=[bsc2])

                def G4(sc2=sc2, bsc2=bsc2, pb_=pb_, bpb_=bpb_):
                    fw.op("vector", lambda v: v.tensor_mul(sc2[:], sc2[:], pb_[:, 0:COLS]),
                          reads=[bsc2, bpb_], writes=[bsc2])

                def G5(ch=ch, sc2=sc2, bsc2=bsc2):
                    fw.op("vector", lambda v: v.tensor_add(
                        lnB[:, ch * COLS:(ch + 1) * COLS], sc2[:], h2[:, (DC + ch) * COLS:(DC + ch + 1) * COLS]),
                        reads=[bsc2, bremote[ch]], writes=[blnB[ch]])

                gstages.append([G0, G1, G2, G3, G4, G5])
            for step in range(len(gstages) + 5):
                for st in range(5, -1, -1):
                    g_ = step - st
                    if 0 <= g_ < len(gstages):
                        gstages[g_][st]()
            if mid2 is not None:
                mid2()
            yb, byb = y[i % NYB], bY[i % NYB]
            for half in range(2):
                sl, bs, _ = stream("outw", half * 4096, (half + 1) * 4096)
                for mcl in range(4):
                    mc = half * 4 + mcl
                    pp, bpp = (pC, bPC) if mc % 2 == 0 else (pD, bPD)
                    for kc in range(DC):
                        fw.op("tensor", lambda t, sl=sl, mcl=mcl, kc=kc, pp=pp: t.matmul(
                            pp[:, 0:COLS], sl[:, mcl * D + kc * 128: mcl * D + kc * 128 + 128],
                            lnB[:, kc * COLS:(kc + 1) * COLS],
                            start=(kc == 0), stop=False),
                            reads=[bs, blnB[kc]], writes=[bpp], signal=False)
                    fw.op("tensor", lambda t, mc=mc, pp=pp: t.matmul(
                        pp[:, 0:COLS], bias_bf[0:1, B_OFFS["outw"] + mc * 128: B_OFFS["outw"] + mc * 128 + 128],
                        ones1b[0:1, 0:COLS], start=False, stop=True),
                        reads=[bbias, bones1b], writes=[bpp])
                    fw.op("vector", lambda v, mc=mc, pp=pp: v.tensor_add(
                        yb[:, mc * COLS:(mc + 1) * COLS], yb[:, mc * COLS:(mc + 1) * COLS], pp[:, 0:COLS]),
                        reads=[bpp, byb[mc]], writes=[byb[mc]])
                    if post_chunk is not None:
                        post_chunk(mc)
            fw.op("vector", lambda v: v.tensor_copy(
                ktail.ap().rearrange("p (c s k) -> p c s k", c=DC, s=SEQ),
                kb.ap().rearrange("p (c s pp) -> p c s pp", c=DC, s=SEQ)[:, :, :, 128:256]),
                reads=[bkb], writes=[bktail])
            fw.op("vector", lambda v: v.tensor_copy(
                vtail.ap().rearrange("p (s d) -> p s d", s=SEQ),
                vb.ap().rearrange("p (s two d) -> p s two d", s=SEQ, two=2)[:, :, 1]),
                reads=[bvb], writes=[bvtail])

        bqb_l = blnF1
        bkb_l = [bkb] * DC

        def mk_passA(i):
            yb, byb = y[i % NYB], bY[i % NYB]
            off = i * DC * COLS

            def store(ch, yb=yb, byb=byb, off=off):
                fw.op("gpd", lambda g, ch=ch: g.dma_start(
                    out=y_d[:, off + ch * COLS: off + (ch + 1) * COLS],
                    in_=yb[:, ch * COLS:(ch + 1) * COLS]),
                    reads=[byb[ch]], writes=[bydram], dma=True)
            return {"src": lnA, "bsrc": blnA, "yb": yb, "byb": byb,
                    "h": h, "bh": bh, "rj": 0, "post_chunk": store,
                    "pre": lambda i=i: ln_p2(i, lnA, blnA)}

        def mk_passB(i):
            return {"src": lnF1, "bsrc": blnF1, "yb": y[i % NYB], "byb": bY[i % NYB],
                    "h": h2, "bh": bh2, "rj": 1,
                    "post_chunk": lambda ch, i=i: ln_p1(i, lnB, blnB, ch)}

        load_x(0)
        ln_pass(0, lnF1, blnF1)
        ffn_multi([mk_passB(0)])
        for i in range(NT):
            if i + 1 < NT:
                load_x(i + 1)

            def mid(i=i):
                if i + 1 < NT:
                    for ch in range(DC):
                        ln_p1(i + 1, lnF1, blnF1, ch)

            def mid2(i=i):
                if i + 1 < NT:
                    ln_p2(i + 1, lnF1, blnF1)
            attn(i, pre=lambda i=i: ln_p2(i, lnB, blnB),
                 post_chunk=lambda ch, i=i: ln_p1(i, lnA, blnA, ch),
                 mid=mid, mid2=mid2)
            if i == 0:
                slot_state["f32"] = False
            plist = []
            if i + 1 < NT:
                plist.append(mk_passB(i + 1))
            plist.append(mk_passA(i))
            ffn_multi(plist)

        fw.finalize(stack)
    return nc


def kernel(**inputs):
    in_maps = _host_layouts(inputs)
    nc = build_program(_PHASES)
    res = run_bass_kernel_spmd(nc, in_maps, core_ids=list(range(NCORES)))
    return _unshard_output(res)


# version 24
# speedup vs baseline: 1.1006x; 1.1006x over previous
import contextlib
import numpy as np
import ml_dtypes
import concourse.bass as bass
import concourse.mybir as mybir
from concourse.bass_utils import run_bass_kernel_spmd

F32 = mybir.dt.float32
BF16 = mybir.dt.bfloat16
AF = mybir.ActivationFunctionType
ALU = mybir.AluOpType
AX = mybir.AxisListType

NCORES = 8
SEQ = 2
S = 4096
D = 1024
DC = 8
I = 2048
IC = 16
NE = 8
NH = 16
HD = 64
TP = 256
NT = S // TP
COLS = SEQ * TP
NLAT = 64
LD = 256
NEG = -1e9

_PHASES = ("ffn1", "attn", "ffn2")

ENGINES = ("tensor", "vector", "scalar", "gpsimd", "gpd", "sync")


class Buf:
    __slots__ = ("name", "writer", "readers")

    def __init__(self, name):
        self.name = name
        self.writer = None
        self.readers = []


class FW:

    def __init__(self, nc):
        self.nc = nc
        self.q = {e: [] for e in ENGINES}
        self.cnt = {e: 0 for e in ENGINES}
        self.hw = {e: {f: 0 for f in ENGINES} for e in ENGINES}
        self.sems = {}
        self.n_waits = 0

    def buf(self, name):
        return Buf(name)

    def op(self, eng, fn, reads=(), writes=(), deps=(), dma=False, signal=True):
        inc = 16 if dma else 1
        need = list(deps)
        for b in reads:
            if b.writer is not None:
                need.append(b.writer)
        for b in writes:
            if b.writer is not None:
                need.append(b.writer)
            need.extend(b.readers)
        per = {}
        for (e2, v) in need:
            assert v <= self.cnt[e2] + (0 if e2 != eng else 1), f"future dep {e2}:{v}"
            if e2 == eng and v > self.cnt[e2]:
                continue
            if per.get(e2, 0) < v:
                per[e2] = v
        waits = []
        for e2 in sorted(per):
            v = per[e2]
            if self.hw[eng][e2] >= v:
                continue
            self.hw[eng][e2] = v
            waits.append((e2, v))
        self.n_waits += len(waits)
        if signal:
            tok = (eng, self.cnt[eng] + inc)
            self.cnt[eng] += inc
        else:
            tok = (eng, self.cnt[eng] + 1)

        def closure(engobj, fw=self, waits=tuple(waits), fn=fn, inc=inc, eng=eng, signal=signal):
            for (e2, v) in waits:
                engobj.wait_ge(fw.sems[e2], v)
            ins = fn(engobj)
            if signal:
                ins.then_inc(fw.sems[eng], inc)

        self.q["gpsimd" if eng == "gpd" else eng].append(closure)
        for b in reads:
            b.readers.append(tok)
        for b in writes:
            b.writer = tok
            b.readers = []
        return tok

    def prewait(self, eng, reads=(), writes=(), deps=()):
        need = list(deps)
        for b in reads:
            if b.writer is not None:
                need.append(b.writer)
        for b in writes:
            if b.writer is not None:
                need.append(b.writer)
            need.extend(b.readers)
        per = {}
        for (e2, v) in need:
            if e2 == eng and v > self.cnt[e2]:
                continue
            if per.get(e2, 0) < v:
                per[e2] = v
        waits = []
        for e2 in sorted(per):
            v = per[e2]
            if self.hw[eng][e2] >= v:
                continue
            self.hw[eng][e2] = v
            waits.append((e2, v))
        if not waits:
            return
        self.n_waits += len(waits)

        def closure(engobj, fw=self, waits=tuple(waits)):
            for (e2, v) in waits:
                engobj.wait_ge(fw.sems[e2], v)

        self.q["gpsimd" if eng == "gpd" else eng].append(closure)

    def finalize(self, stack):
        nc = self.nc
        for e in ENGINES:
            self.sems[e] = stack.enter_context(nc.semaphore(f"sem_{e}"))
        block = stack.enter_context(nc.Block())
        fw = self

        def emit(engobj, e):
            for c in fw.q[e]:
                c(engobj)
        assert not fw.q["gpd"]

        @block.tensor
        def _(t):
            emit(t, "tensor")

        @block.vector
        def _(v):
            emit(v, "vector")

        @block.scalar
        def _(s):
            emit(s, "scalar")

        @block.gpsimd
        def _(g):
            emit(g, "gpsimd")
            for e in ENGINES:
                if fw.cnt[e] > 0:
                    g.wait_ge(fw.sems[e], fw.cnt[e])

        @block.sync
        def _(s):
            emit(s, "sync")


_W_SPECS = [
    ("qkv", D, 3 * D),
    ("rq", D, LD),
    ("rkv", D, 2 * LD),
    ("rout", LD, D),
    ("outw", D, D),
    ("wr", D, 8),
    ("gs", D, I), ("us", D, I), ("ds", I, D),
]
for _e in range(NE):
    _W_SPECS += [(f"g{_e}", D, I), (f"u{_e}", D, I), (f"d{_e}", I, D)]


def _w_offsets():
    offs, off = {}, 0
    for name, K, N in _W_SPECS:
        offs[name] = (off, K, N)
        off += (K // 128) * N
    return offs, off


W_OFFS, W_COLS = _w_offsets()

B_OFFS = {"qkv": 0, "rq": 3072, "rkv": 3328, "rout": 3840, "outw": 4864, "rb": 5888}
B_COLS = 5896

C_M256, C_M128, C_LMASK, C_ID, C_GL, C_GR, C_SEL2, C_SELE = 0, 256, 384, 512, 640, 642, 644, 772
C_COLS = 772 + 8 * 128
CF_COLS = 8


def _host_layouts(ins):
    x = np.ascontiguousarray(ins["L0_attn_x"])

    mats = {
        "qkv": np.ascontiguousarray(ins["L0_attn_qkv_w"]),
        "rq": np.ascontiguousarray(ins["L0_attn_rq_w"]),
        "rkv": np.ascontiguousarray(ins["L0_attn_rkv_w"]),
        "rout": np.ascontiguousarray(ins["L0_attn_rout_w"]),
        "outw": np.ascontiguousarray(ins["L0_attn_out_w"]),
        "wr": np.ascontiguousarray(ins["L0_ffn_Wr"].T),
        "gs": np.ascontiguousarray(ins["L0_ffn_Wg_s"].T),
        "us": np.ascontiguousarray(ins["L0_ffn_Wu_s"].T),
        "ds": np.ascontiguousarray(ins["L0_ffn_Wd_s"].T),
    }
    for e in range(NE):
        mats[f"g{e}"] = np.ascontiguousarray(ins["L0_ffn_Wg"][e].T)
        mats[f"u{e}"] = np.ascontiguousarray(ins["L0_ffn_Wu"][e].T)
        mats[f"d{e}"] = np.ascontiguousarray(ins["L0_ffn_Wd"][e].T)

    wf = np.empty((128, W_COLS), np.float32)
    for name, (off, K, N) in W_OFFS.items():
        W = mats[name]
        KC = K // 128
        if N >= 128:
            MC = N // 128
            arr = W.reshape(KC, 128, MC, 128).transpose(1, 2, 0, 3).reshape(128, KC * N)
        else:
            arr = W.reshape(KC, 128, N).transpose(1, 0, 2).reshape(128, KC * N)
        wf[:, off:off + KC * N] = arr

    bia = np.zeros((1, B_COLS), np.float32)
    bia[0, 0:3072] = ins["L0_attn_qkv_b"]
    bia[0, 3072:3328] = ins["L0_attn_rq_b"]
    bia[0, 3328:3840] = ins["L0_attn_rkv_b"]
    bia[0, 3840:4864] = ins["L0_attn_rout_b"]
    bia[0, 4864:5888] = ins["L0_attn_out_b"]
    bia[0, 5888:5896] = ins["L0_ffn_rb"]

    cst = np.zeros((128, C_COLS), np.float32)
    i_ = np.arange(128)[:, None]
    j_ = np.arange(256)[None, :]
    cst[:, C_M256:C_M256 + 256] = np.where((j_ >= i_) & (j_ <= i_ + 127), 0.0, NEG)
    j2 = np.arange(128)[None, :]
    cst[:, C_M128:C_M128 + 128] = np.where(j2 <= i_ - 1, 0.0, NEG)
    lm = np.where(j2 <= 63, 0.0, NEG)
    lm = np.where((i_ >= 64) & (j2 == 64), 0.0, lm)
    cst[:, C_LMASK:C_LMASK + 128] = lm
    cst[:, C_ID:C_ID + 128] = np.eye(128, dtype=np.float32)
    gl = ins["L0_attn_gl_w"].reshape(64)
    gr = ins["L0_attn_gr_w"].reshape(64)
    cst[0:64, C_GL] = gl
    cst[64:128, C_GL + 1] = gl
    cst[0:64, C_GR] = gr
    cst[64:128, C_GR + 1] = gr
    cst[0, C_SEL2:C_SEL2 + 64] = 1.0
    cst[1, C_SEL2 + 64:C_SEL2 + 128] = 1.0
    for e in range(NE):
        cst[e, C_SELE + e * 128:C_SELE + (e + 1) * 128] = 1.0

    cstf = np.zeros((128, CF_COLS), np.float32)
    cstf[:, 0:4] = ins["L0_attn_rkv_b"].reshape(4, 128).T
    cstf[0, 4] = 1e-5

    in_maps = []
    for c in range(NCORES):
        xc = x[SEQ * c:SEQ * (c + 1)]
        arr = xc.reshape(SEQ, NT, TP, DC, 128).transpose(4, 1, 3, 0, 2).reshape(128, NT * DC * COLS)
        in_maps.append({
            "x": np.ascontiguousarray(arr),
            "wf": wf, "bia": bia, "cst": cst, "cstf": cstf,
        })
    return in_maps


def _unshard_output(res):
    outs = []
    for c in range(NCORES):
        y = np.asarray(res.results[c]["y"], np.float32)
        a = y.reshape(128, NT, DC, SEQ, TP).transpose(3, 1, 4, 2, 0).reshape(SEQ, S, D)
        outs.append(a)
    return np.concatenate(outs, 0)


def build_program(phases=None):
    if phases is None:
        phases = _PHASES
    nc = bass.Bass("TRN2", target_bir_lowering=False)
    x_d = nc.dram_tensor("x", [128, NT * DC * COLS], F32, kind="ExternalInput").ap()
    wf_d = nc.dram_tensor("wf", [128, W_COLS], F32, kind="ExternalInput").ap()
    bia_d = nc.dram_tensor("bia", [1, B_COLS], F32, kind="ExternalInput").ap()
    cst_d = nc.dram_tensor("cst", [128, C_COLS], F32, kind="ExternalInput").ap()
    cstf_d = nc.dram_tensor("cstf", [128, CF_COLS], F32, kind="ExternalInput").ap()
    y_d = nc.dram_tensor("y", [128, NT * DC * COLS], F32, kind="ExternalOutput").ap()
    wbf_d = nc.dram_tensor("wbf", [128, W_COLS], BF16, kind="Internal").ap()

    stack = contextlib.ExitStack()
    with stack:
        SB = lambda n, sh, dt: stack.enter_context(nc.sbuf_tensor(n, sh, dt))
        PS = lambda n, sh, dt: stack.enter_context(nc.psum_tensor(n, sh, dt))

        NYB = 2
        NWS = 4
        y = [SB(f"y{j}", [128, DC * COLS], F32) for j in range(NYB)]
        lnA = SB("lnA", [128, DC * COLS], BF16)
        lnB = SB("lnB", [128, DC * COLS], BF16)
        h = SB("h", [128, IC * COLS], BF16)
        h2 = SB("h2", [128, IC * COLS], BF16)
        cwb = SB("cwb", [128, 4 * COLS], BF16)
        sg = [SB(f"sg{j}", [128, COLS], F32) for j in range(2)]
        sqf = SB("sqf", [128, 2 * COLS], BF16)
        scr = [SB(f"scr{j}", [128, COLS], F32) for j in range(2)]
        gscr = SB("gscr", [128, COLS], F32)
        wring = [SB(f"w{i}", [128, 4096], BF16) for i in range(NWS)]
        qb = SB("qb", [128, DC * COLS], BF16)
        lnF1 = qb
        kb = SB("kb", [128, DC * COLS], BF16)
        ktail = SB("ktail", [128, DC * SEQ * 128], BF16)
        vb = SB("vb", [128, 4 * D], BF16)
        vtail = SB("vtail", [128, SEQ * D], BF16)
        rqb = SB("rqb", [128, 2 * COLS], BF16)
        pooled = SB("pooled", [128, DC * SEQ * NLAT], BF16)
        rkbuf = SB("rkbuf", [128, 2 * 128], BF16)
        rvfbuf = SB("rvfbuf", [128, 2 * 128], BF16)
        rvlat = SB("rvlat", [128, LD], BF16)
        latout = SB("latout", [128, 2 * COLS], BF16)
        pbf = SB("pbf", [128, 4 * 256], BF16)
        attnT = SB("attnT", [128, 4 * 256], BF16)
        lattnT = SB("lattnT", [128, 2 * 128], BF16)
        mx = SB("mx", [128, 8], F32)
        den = SB("den", [128, 8], F32)
        rden = SB("rden", [128, 8], F32)
        gatebf = SB("gatebf", [2, 2 * COLS], BF16)
        stats = SB("stats", [1, 4 * COLS], F32)
        probs = [SB(f"probs{j}", [128, 4 * NE], F32) for j in range(2)]
        top2 = [SB(f"top2{j}", [128, 64], F32) for j in range(2)]
        cwtok = [SB(f"cwtok{j}", [128, 4 * NE], F32) for j in range(2)]
        cwtb = [SB(f"cwtb{j}", [128, 4 * NE], BF16) for j in range(2)]
        cwf = [SB(f"cwf{j}", [8, COLS], BF16) for j in range(2)]
        bias_bf = SB("bias_bf", [1, B_COLS], BF16)
        cstb = SB("cstb", [128, C_COLS], BF16)
        cstf_s = SB("cstf_s", [128, CF_COLS], F32)
        ones1b = SB("ones1b", [1, 512], BF16)
        ones128b = SB("ones128b", [128, 1], BF16)
        ones128f = SB("ones128f", [128, 1], F32)
        ones1f = SB("ones1f", [1, 128], F32)
        wrres = SB("wrres", [128, 64], BF16)

        pA = PS("pA", [128, 512], F32)
        pB = PS("pB", [128, 512], F32)
        pC = PS("pC", [128, 512], F32)
        pD = PS("pD", [128, 512], F32)
        pE = PS("pE", [128, 512], F32)
        pF = PS("pF", [128, 512], F32)
        pT = PS("pT", [128, 1024], BF16)
        pT2 = PS("pT2", [128, 1024], BF16)

        fw = FW(nc)
        bufs = {}

        def B(name):
            if name not in bufs:
                bufs[name] = fw.buf(name)
            return bufs[name]

        def BL(name, n):
            return [B(f"{name}.{j}") for j in range(n)]

        bY = [BL(f"y{j}", DC) for j in range(NYB)]
        blnA = BL("lnA", DC)
        blnB = BL("lnB", DC)
        blnF1 = BL("lnF1", DC)
        bh = BL("h", IC)
        bh2 = BL("h2", IC)
        bcwb = [B(f"cwb{j}") for j in range(4)]
        bsg = [B(f"sg{j}") for j in range(2)]
        bsqf = BL("sqf", 2)
        bscr = [B(f"scr{j}") for j in range(2)]
        bgscr = B("gscr")
        bwr = [B(f"w{i}") for i in range(NWS)]
        bkb, bktail, bvb, bvtail = B("kb"), B("ktail"), B("vb"), B("vtail")
        brqb, bpooled, brk, brvf, brvlat = B("rqb"), B("pooled"), B("rkbuf"), B("rvfbuf"), B("rvlat")
        blatout = B("latout")
        blocal = bh2[0:DC]
        bremote = bh2[DC:]
        bpbf = BL("pbf", 4)
        battnT = BL("attnT", 4)
        blattnT = B("lattnT")
        bmx = BL("mx", 8)
        bden = BL("den", 8)
        brden = BL("rden", 8)
        bgate, bstats = B("gatebf"), B("stats")
        bprobs = [B(f"probs{j}") for j in range(2)]
        btop2 = [B(f"top2{j}") for j in range(2)]
        bcwtok = [B(f"cwtok{j}") for j in range(2)]
        bcwtb = [B(f"cwtb{j}") for j in range(2)]
        bcwf = [B(f"cwf{j}") for j in range(2)]
        bbias, bcst, bcstf = B("bias_bf"), B("cstb"), B("cstf_s")
        bones1b, bones128b, bones1f = B("ones1b"), B("ones128b"), B("ones1f")
        bones128f = B("ones128f")
        bwrres = B("wrres")
        bPA, bPB, bPC, bPD, bPE, bPF, bPT, bPT2 = (B(n) for n in
            ("pA", "pB", "pC", "pD", "pE", "pF", "pT", "pT2"))
        NWCH = (W_COLS + 4095) // 4096
        bwbf = BL("wbf", NWCH)
        bydram = B("ydram")

        fw.op("gpd", lambda g: g.dma_start(out=bias_bf[:], in_=bia_d), writes=[bbias], dma=True)
        fw.op("gpd", lambda g: g.dma_start(out=cstb[:], in_=cst_d), writes=[bcst], dma=True)
        fw.op("gpd", lambda g: g.dma_start(out=cstf_s[:], in_=cstf_d), writes=[bcstf], dma=True)
        fw.op("gpsimd", lambda g: g.memset(ones1b[:], 1.0), writes=[bones1b])
        fw.op("gpsimd", lambda g: g.memset(ones128b[:], 1.0), writes=[bones128b])
        fw.op("gpsimd", lambda g: g.memset(ones128f[:], 1.0), writes=[bones128f])
        fw.op("gpsimd", lambda g: g.memset(ones1f[:], 1.0), writes=[bones1f])
        fw.op("gpsimd", lambda g: g.memset(pooled[:], 0.0), writes=[bpooled])
        fw.op("gpsimd", lambda g: g.memset(rkbuf[:], 0.0), writes=[brk])
        fw.op("gpsimd", lambda g: g.memset(rvfbuf[:], 0.0), writes=[brvf])
        fw.op("gpsimd", lambda g: g.memset(rvlat[:], 0.0), writes=[brvlat])

        def load_x(i):
            yb, byb = y[i % NYB], bY[i % NYB]
            off = i * DC * COLS
            fw.op("gpd", lambda g, yb=yb, off=off: g.dma_start(out=yb[:], in_=x_d[:, off:off + DC * COLS]),
                  writes=byb, dma=True)

        woff = W_OFFS["wr"][0]
        fw.op("gpd", lambda g: g.dma_start(out=wrres[:], in_=wf_d[:, woff:woff + 64]),
              writes=[bwrres], dma=True)
        load_x(0)

        slot_state = {"i": 0, "f32": True}

        def stream(name, cols_a, cols_b):
            off, K, N = W_OFFS[name]
            si = slot_state["i"] % NWS
            slot_state["i"] += 1
            sl, bs = wring[si], bwr[si]
            a, b = off + cols_a, off + cols_b
            rng = list(range(a // 4096, (b - 1) // 4096 + 1))
            if slot_state["f32"]:
                fw.op("gpd", lambda g, sl=sl, a=a, b=b: g.dma_start(out=sl[:, 0:b - a], in_=wf_d[:, a:b]),
                      writes=[bs], dma=True)
                fw.op("sync", lambda s, sl=sl, a=a, b=b: s.dma_start(out=wbf_d[:, a:b], in_=sl[:, 0:b - a]),
                      reads=[bs], writes=[bwbf[j] for j in rng], dma=True)
                return sl, bs, cols_a
            fw.op("sync", lambda s, sl=sl, a=a, b=b: s.dma_start(out=sl[:, 0:b - a], in_=wbf_d[:, a:b]),
                  reads=[bwbf[j] for j in rng], writes=[bs], dma=True)
            return sl, bs, cols_a

        def CC(ch, s=0, p0=0):
            return ch * COLS + s * TP + p0

        def ln_p1(i, dst, bdst, ch):
            yb, byb = y[i % NYB], bY[i % NYB]
            fw.op("vector", lambda v, ch=ch: v.tensor_copy(dst[:, CC(ch):CC(ch) + COLS], yb[:, CC(ch):CC(ch) + COLS]),
                  reads=[byb[ch]], writes=[bdst[ch]])
            fw.op("tensor", lambda t, ch=ch: t.matmul(pA[0:1, 0:COLS], ones128b[:], dst[:, CC(ch):CC(ch) + COLS],
                                                      start=(ch == 0), stop=(ch == DC - 1), skip_group_check=True),
                  reads=[bones128b, bdst[ch]], writes=[bPA])
            sqs = (ch % 2) * COLS
            fw.op("scalar", lambda sx, ch=ch, sqs=sqs: sx.activation(sqf[:, sqs:sqs + COLS], yb[:, CC(ch):CC(ch) + COLS], AF.Square),
                  reads=[byb[ch]], writes=[bsqf[ch % 2]])
            fw.op("tensor", lambda t, ch=ch, sqs=sqs: t.matmul(pB[0:1, 0:COLS], ones128b[:], sqf[:, sqs:sqs + COLS],
                                                      start=(ch == 0), stop=(ch == DC - 1), skip_group_check=True),
                  reads=[bones128b, bsqf[ch % 2]], writes=[bPB])

        def spacer(tok):
            fw.op("tensor", lambda t: t.matmul(
                pB[0:1, 0:128], ones128b[0:1, 0:1], ones1b[0:1, 0:128],
                start=True, stop=True, skip_group_check=True),
                reads=[bones128b, bones1b], writes=[bPB], deps=[tok])

        def ln_p2(i, dst, bdst):
            yb, byb = y[i % NYB], bY[i % NYB]
            m_ = lambda: stats[0:1, 0:COLS]
            m2_ = lambda: stats[0:1, COLS:2 * COLS]
            sqm_ = lambda: stats[0:1, 2 * COLS:3 * COLS]
            rstd_ = lambda: stats[0:1, 3 * COLS:4 * COLS]
            mrs_ = m2_
            var_ = m2_
            fw.op("scalar", lambda sx: sx.activation(m_(), pA[0:1, 0:COLS], AF.Copy, scale=1.0 / D),
                  reads=[bPA], writes=[bstats])
            tk = fw.op("scalar", lambda sx: sx.activation(sqm_(), pB[0:1, 0:COLS], AF.Copy, scale=1.0 / D),
                  reads=[bPB], writes=[bstats])
            spacer(tk)
            tk = fw.op("scalar", lambda sx: sx.activation(m2_(), m_(), AF.Square),
                  reads=[bstats], writes=[bstats])
            spacer(tk)
            tk = fw.op("vector", lambda v: v.tensor_sub(var_(), sqm_(), m2_()),
                  reads=[bstats], writes=[bstats])
            spacer(tk)
            tk = fw.op("scalar", lambda sx: sx.activation(sqm_(), var_(), AF.Ln, bias=cstf_s[0:1, 4:5]),
                  reads=[bstats, bcstf], writes=[bstats])
            spacer(tk)
            tk = fw.op("scalar", lambda sx: sx.activation(rstd_(), sqm_(), AF.Exp, scale=-0.5),
                  reads=[bstats], writes=[bstats])
            spacer(tk)
            fw.op("vector", lambda v: v.tensor_mul(mrs_(), m_(), rstd_()),
                  reads=[bstats], writes=[bstats])
            fw.op("tensor", lambda t: t.matmul(pE[:, 0:COLS], ones1f[0:1, 0:128], rstd_(), start=True, stop=True),
                  reads=[bones1f, bstats], writes=[bPE])
            fw.op("tensor", lambda t: t.matmul(pF[:, 0:COLS], ones1f[0:1, 0:128], mrs_(), start=True, stop=True),
                  reads=[bones1f, bstats], writes=[bPF])
            for ch in range(DC):
                sc, bsc = scr[ch % 2], bscr[ch % 2]
                fw.op("vector", lambda v, ch=ch, sc=sc: v.tensor_mul(sc[:], yb[:, CC(ch):CC(ch) + COLS], pE[:, 0:COLS]),
                      reads=[byb[ch], bPE], writes=[bsc])
                tk = fw.op("vector", lambda v, ch=ch, sc=sc: v.tensor_sub(dst[:, CC(ch):CC(ch) + COLS], sc[:], pF[:, 0:COLS]),
                      reads=[bsc, bPF], writes=[bdst[ch]])
                if ch % 2 == 1:
                    spacer(tk)

        def ln_pass(i, dst, bdst):
            for ch in range(DC):
                ln_p1(i, dst, bdst, ch)
            ln_p2(i, dst, bdst)

        def routing(src, bsrc, rj):
            prb, tpb, ctk, ctb, cwfb = probs[rj], top2[rj], cwtok[rj], cwtb[rj], cwf[rj]
            bprb, btpb, bctk, bctb, bcwfb = bprobs[rj], btop2[rj], bcwtok[rj], bcwtb[rj], bcwf[rj]
            for kc in range(DC):
                for tc in range(4):
                    fw.op("tensor", lambda t, tc=tc, kc=kc: t.matmul(
                        pD[0:128, tc * 8:tc * 8 + 8],
                        src[:, kc * COLS + tc * 128: kc * COLS + tc * 128 + 128],
                        wrres[:, kc * 8:kc * 8 + 8],
                        start=(kc == 0 and tc == 0), stop=False, skip_group_check=True),
                        reads=[bsrc[kc], bwrres], writes=[bPD], signal=False)
            for tc in range(4):
                fw.op("tensor", lambda t, tc=tc: t.matmul(
                    pD[0:128, tc * 8:tc * 8 + 8], ones1b[0:1, 0:128],
                    bias_bf[0:1, B_OFFS["rb"]:B_OFFS["rb"] + 8],
                    start=False, stop=(tc == 3), skip_group_check=True),
                    reads=[bones1b, bbias], writes=[bPD], signal=(tc == 3))
            fw.op("scalar", lambda s: s.activation(prb[:], pD[0:128, 0:32], AF.Sigmoid),
                  reads=[bPD], writes=[bprb])
            for tc in range(4):
                pr = prb[:, tc * 8:(tc + 1) * 8]
                m1 = tpb[:, tc * 16:tc * 16 + 1]
                ge1 = tpb[:, tc * 16 + 2:tc * 16 + 10]
                cnt = tpb[:, tc * 16 + 10:tc * 16 + 11]
                strict = tpb[:, tc * 16 + 11:tc * 16 + 12]
                cnt2 = tpb[:, tc * 16 + 12:tc * 16 + 13]
                dd = tpb[:, tc * 16 + 13:tc * 16 + 14]
                th = tpb[:, tc * 16 + 14:tc * 16 + 15]
                fw.op("vector", lambda v, pr=pr, m1=m1: v.reduce_max(m1, pr, axis=AX.X), reads=[bprb], writes=[btpb])
                fw.op("vector", lambda v, pr=pr, ge1=ge1, m1=m1: v.tensor_scalar(ge1, pr, m1, None, ALU.is_ge),
                      reads=[bprb, btpb], writes=[btpb])
                fw.op("vector", lambda v, ge1=ge1, cnt=cnt: v.reduce_sum(cnt, ge1, axis=AX.X), reads=[btpb], writes=[btpb])
                fw.op("vector", lambda v, pr=pr, ge1=ge1, tc=tc: v.scalar_tensor_tensor(
                    ctk[:, tc * 8:(tc + 1) * 8], ge1, -1e30, pr, ALU.mult, ALU.add),
                    reads=[bprb, btpb], writes=[bctk])
                fw.op("vector", lambda v, tc=tc, strict=strict: v.reduce_max(strict, ctk[:, tc * 8:(tc + 1) * 8], axis=AX.X),
                      reads=[bctk], writes=[btpb])
                fw.op("vector", lambda v, cnt=cnt, cnt2=cnt2: v.tensor_scalar(cnt2, cnt, 2.0, None, ALU.is_ge),
                      reads=[btpb], writes=[btpb])
                fw.op("vector", lambda v, m1=m1, strict=strict, dd=dd: v.tensor_sub(dd, m1, strict),
                      reads=[btpb], writes=[btpb])
                fw.op("vector", lambda v, cnt2=cnt2, dd=dd, th=th: v.scalar_tensor_tensor(
                    th, cnt2, 1.0, dd, ALU.mult, ALU.mult), reads=[btpb], writes=[btpb])
                fw.op("vector", lambda v, th=th, strict=strict: v.tensor_add(th, th, strict),
                      reads=[btpb], writes=[btpb])
                fw.op("vector", lambda v, pr=pr, th=th, ge1=ge1: v.tensor_scalar(ge1, pr, th, None, ALU.is_ge),
                      reads=[bprb, btpb], writes=[btpb])
                fw.op("vector", lambda v, pr=pr, ge1=ge1, tc=tc: v.tensor_mul(ctk[:, tc * 8:(tc + 1) * 8], pr, ge1),
                      reads=[bprb, btpb], writes=[bctk])
            fw.op("vector", lambda v: v.tensor_copy(ctb[:], ctk[:]), reads=[bctk], writes=[bctb])

            def emit_cwf():
                for tc in range(4):
                    fw.op("tensor", lambda t, tc=tc: t.transpose(
                        pT[0:8, tc * 128:tc * 128 + 128], ctb[:, tc * 8:(tc + 1) * 8], cstb[:, C_ID:C_ID + 128]),
                        reads=[bctb, bcst], writes=[bPT])
                for tc in range(4):
                    fw.op("vector", lambda v, tc=tc: v.tensor_copy(
                        cwfb[:, tc * 128:(tc + 1) * 128], pT[0:8, tc * 128:tc * 128 + 128]),
                        reads=[bPT], writes=[bcwfb])

            def emit_bc(e):
                sl_ = 2 * rj + e % 2
                pp, bpp = (pE, bPE) if e % 2 == 0 else (pF, bPF)
                fw.op("tensor", lambda t, e=e, pp=pp: t.matmul(
                    pp[:, 0:COLS], cstb[0:8, C_SELE + e * 128:C_SELE + (e + 1) * 128],
                    cwfb[:], start=True, stop=True),
                    reads=[bcst, bcwfb], writes=[bpp])
                fw.op("scalar", lambda s, sl_=sl_, pp=pp: s.activation(
                    cwb[:, sl_ * COLS:(sl_ + 1) * COLS], pp[:, 0:COLS], AF.Copy),
                    reads=[bpp], writes=[bcwb[sl_]])
            return emit_cwf, emit_bc

        def ffn_multi(passes):
            emits = []
            for p in passes:
                if p.get("pre") is not None:
                    p["pre"]()
                emits.append(routing(p["src"], p["bsrc"], p["rj"]))
            experts = [("gs", "us", "ds", None)] + [(f"g{e}", f"u{e}", f"d{e}", e) for e in range(NE)]
            np_ = len(passes)
            for (gn, un, dn, e) in experts:
                grp = 0
                for sc4 in range(4):
                    sl_g, bs_g, _ = stream(gn, sc4 * 4096, (sc4 + 1) * 4096)
                    sl_u, bs_u, _ = stream(un, sc4 * 4096, (sc4 + 1) * 4096)
                    for mcl in range(4):
                        mc = sc4 * 4 + mcl
                        for pi, p in enumerate(passes):
                            src, bsrc = p["src"], p["bsrc"]
                            hbuf, bhbuf = p["h"], p["bh"]
                            pg, bpg = (pA, bPA) if grp % 2 == 0 else (pC, bPC)
                            pu, bpu = (pB, bPB) if grp % 2 == 0 else (pD, bPD)
                            for kc in range(DC):
                                if kc == DC - 1:
                                    fw.prewait("tensor", reads=[bs_u], writes=[bpu])
                                fw.op("tensor", lambda t, sl=sl_g, mcl=mcl, kc=kc, pg=pg, src=src: t.matmul(
                                    pg[:, 0:COLS],
                                    sl[:, mcl * D + kc * 128: mcl * D + kc * 128 + 128],
                                    src[:, kc * COLS:(kc + 1) * COLS],
                                    start=(kc == 0), stop=(kc == DC - 1)),
                                    reads=[bs_g, bsrc[kc]], writes=[bpg], signal=(kc == DC - 1))
                            sgb, bsgb = sg[grp % 2], bsg[grp % 2]
                            fw.op("scalar", lambda s, sgb=sgb, pg=pg: s.activation(sgb[:], pg[:, 0:COLS], AF.Silu),
                                  reads=[bpg], writes=[bsgb])
                            last_gu = (mcl == 3 and pi == np_ - 1)
                            for kc in range(DC):
                                if kc == DC - 1 and not last_gu:
                                    pgn = bPA if (grp + 1) % 2 == 0 else bPC
                                    fw.prewait("tensor", writes=[pgn])
                                fw.op("tensor", lambda t, sl=sl_u, mcl=mcl, kc=kc, pu=pu, src=src: t.matmul(
                                    pu[:, 0:COLS],
                                    sl[:, mcl * D + kc * 128: mcl * D + kc * 128 + 128],
                                    src[:, kc * COLS:(kc + 1) * COLS],
                                    start=(kc == 0), stop=(kc == DC - 1)),
                                    reads=[bs_u, bsrc[kc]], writes=[bpu], signal=(kc == DC - 1))
                            if e is None:
                                fw.op("vector", lambda v, mc=mc, sgb=sgb, pu=pu, hbuf=hbuf: v.tensor_mul(
                                    hbuf[:, mc * COLS:(mc + 1) * COLS], sgb[:], pu[:, 0:COLS]),
                                    reads=[bsgb, bpu], writes=[bhbuf[mc]])
                            else:
                                sc2, bsc2 = scr[grp % 2], bscr[grp % 2]
                                cslot = 2 * p["rj"] + (e % 2)
                                fw.op("vector", lambda v, sgb=sgb, sc2=sc2, pu=pu: v.tensor_mul(sc2[:], sgb[:], pu[:, 0:COLS]),
                                      reads=[bsgb, bpu], writes=[bsc2])
                                fw.op("vector", lambda v, mc=mc, sc2=sc2, hbuf=hbuf, cslot=cslot: v.tensor_mul(
                                    hbuf[:, mc * COLS:(mc + 1) * COLS], sc2[:],
                                    cwb[:, cslot * COLS:(cslot + 1) * COLS]),
                                    reads=[bsc2, bcwb[cslot]], writes=[bhbuf[mc]])
                            grp += 1
                if e is None:
                    for (emit_cwf, emit_bc) in emits:
                        emit_cwf()
                        emit_bc(0)
                        emit_bc(1)
                elif e + 2 < NE:
                    for (emit_cwf, emit_bc) in emits:
                        emit_bc(e + 2)
                dgrp = 0
                for sc2_ in range(4):
                    sl, bs, _ = stream(dn, sc2_ * 4096, (sc2_ + 1) * 4096)
                    for mcl in range(2):
                        oc = sc2_ * 2 + mcl
                        for pi, p in enumerate(passes):
                            hbuf, bhbuf = p["h"], p["bh"]
                            pp, bpp = (pE, bPE) if dgrp % 2 == 0 else (pF, bPF)
                            last_d = (oc == 7 and pi == np_ - 1)
                            for kc in range(IC):
                                if kc == IC - 1 and not last_d:
                                    ppn = bPE if (dgrp + 1) % 2 == 0 else bPF
                                    fw.prewait("tensor", writes=[ppn])
                                fw.op("tensor", lambda t, sl=sl, mcl=mcl, kc=kc, pp=pp, hbuf=hbuf: t.matmul(
                                    pp[:, 0:COLS],
                                    sl[:, mcl * I + kc * 128: mcl * I + kc * 128 + 128],
                                    hbuf[:, kc * COLS:(kc + 1) * COLS],
                                    start=(kc == 0), stop=(kc == IC - 1)),
                                    reads=[bs, bhbuf[kc]], writes=[bpp], signal=(kc == IC - 1))
                            yb, byb = p["yb"], p["byb"]
                            fw.op("vector", lambda v, oc=oc, pp=pp, yb=yb: v.scalar_tensor_tensor(
                                yb[:, oc * COLS:(oc + 1) * COLS], pp[:, 0:COLS], 0.5,
                                yb[:, oc * COLS:(oc + 1) * COLS], ALU.mult, ALU.add),
                                reads=[bpp, byb[oc]], writes=[byb[oc]])
                            if e == NE - 1 and p.get("post_chunk") is not None:
                                p["post_chunk"](oc)
                            dgrp += 1

        def fm_proj(name, src, bsrc, dst, bdst, mc_range, bias_off, scale=1.0):
            off, K, N = W_OFFS[name]
            KC = K // 128
            mc0 = mc_range[0]
            per_slot = 4096 // K
            idx = 0
            while idx < len(mc_range):
                group = mc_range[idx:idx + per_slot]
                sl, bs, _ = stream(name, group[0] * K, (group[-1] + 1) * K)
                for gi, mc in enumerate(group):
                    pp, bpp = (pA, bPA) if (mc % 2 == 0) else (pB, bPB)
                    for kc in range(KC):
                        fw.op("tensor", lambda t, sl=sl, gi=gi, kc=kc, pp=pp, K=K: t.matmul(
                            pp[:, 0:COLS],
                            sl[:, gi * K + kc * 128: gi * K + kc * 128 + 128],
                            src[:, kc * COLS:(kc + 1) * COLS],
                            start=(kc == 0), stop=False),
                            reads=[bs, bsrc[kc]], writes=[bpp], signal=False)
                    fw.op("tensor", lambda t, mc=mc, pp=pp: t.matmul(
                        pp[:, 0:COLS], bias_bf[0:1, bias_off + mc * 128: bias_off + mc * 128 + 128],
                        ones1b[0:1, 0:COLS], start=False, stop=True),
                        reads=[bbias, bones1b], writes=[bpp])
                    dc_ = mc - mc0
                    fw.op("scalar", lambda s, dc_=dc_, pp=pp, scale=scale: s.activation(
                        dst[:, dc_ * COLS:(dc_ + 1) * COLS], pp[:, 0:COLS], AF.Copy, scale=scale),
                        reads=[bpp], writes=[bdst if not isinstance(bdst, list) else bdst[dc_]])
                idx += per_slot

        def attn(i, pre=None, post_chunk=None, mid=None, mid2=None):
            src, bsrc = lnB, blnB
            if pre is None:
                ln_pass(i, src, bsrc)
            else:
                pre()
            fm_proj("qkv", src, bsrc, qb, bqb_l, list(range(0, 8)), B_OFFS["qkv"], scale=0.125)
            fm_proj("qkv", src, bsrc, kb, bkb_l, list(range(8, 16)), B_OFFS["qkv"])
            for half in range(2):
                sl, bs, _ = stream("qkv", (16 + 4 * half) * D, (20 + 4 * half) * D)
                for tc in range(4):
                    pp, bpp = (pA, bPA) if tc % 2 == 0 else (pB, bPB)
                    for kc in range(DC):
                        fw.op("tensor", lambda t, sl=sl, tc=tc, kc=kc, pp=pp: t.matmul(
                            pp[:, 0:512],
                            src[:, kc * COLS + tc * 128: kc * COLS + tc * 128 + 128],
                            sl.rearrange("p (mc k) -> p mc k", k=D)[:, :, kc * 128:kc * 128 + 128],
                            start=(kc == 0), stop=False),
                            reads=[bsrc[kc], bs], writes=[bpp], signal=False)
                    fw.op("tensor", lambda t, half=half, tc=tc, pp=pp: t.matmul(
                        pp[:, 0:512], ones1b[0:1, 0:128],
                        bias_bf[0:1, B_OFFS["qkv"] + 2048 + half * 512: B_OFFS["qkv"] + 2048 + half * 512 + 512],
                        start=False, stop=True),
                        reads=[bones1b, bbias], writes=[bpp])
                    fw.op("scalar", lambda s, half=half, tc=tc, pp=pp: s.activation(
                        vb[:, tc * D + half * 512: tc * D + half * 512 + 512], pp[:, 0:512], AF.Copy),
                        reads=[bpp], writes=[bvb])
            fm_proj("rq", src, bsrc, rqb, brqb, [0, 1], B_OFFS["rq"], scale=0.125)
            for ch in range(DC):
                for s_ in range(SEQ):
                    dst = pooled[:, ch * 128 + s_ * 64 + 4 * i: ch * 128 + s_ * 64 + 4 * i + 4]
                    srcv = src.ap().rearrange("p (c8 s2 a b) -> p c8 s2 a b", c8=DC, s2=SEQ, a=4)[:, ch, s_]
                    sc2, bsc2 = scr[ch % 2], bscr[ch % 2]
                    fw.op("vector", lambda v, srcv=srcv, sc2=sc2: v.reduce_sum(
                        sc2.ap().rearrange("p (a b) -> p a b", a=4)[:, :, 0:1], srcv, axis=AX.X),
                        reads=[bsrc[ch]], writes=[bsc2])
                    fw.op("vector", lambda v, dst=dst, sc2=sc2: v.tensor_copy(dst, sc2[:, 0:4]),
                        reads=[bsc2], writes=[bpooled])
            sl, bs, _ = stream("rkv", 0, 4096)
            for mc in range(4):
                pp, bpp = (pA, bPA) if mc % 2 == 0 else (pB, bPB)
                for kc in range(DC):
                    fw.op("tensor", lambda t, sl=sl, mc=mc, kc=kc, pp=pp: t.matmul(
                        pp[:, 0:128],
                        sl[:, mc * D + kc * 128: mc * D + kc * 128 + 128],
                        pooled[:, kc * 128:(kc + 1) * 128],
                        start=(kc == 0), stop=(kc == DC - 1)),
                        reads=[bs, bpooled], writes=[bpp], signal=(kc == DC - 1))
                dstb, bdstb = (rkbuf, brk) if mc < 2 else (rvfbuf, brvf)
                fw.op("vector", lambda v, mc=mc, pp=pp, dstb=dstb: v.tensor_scalar(
                    dstb[:, (mc % 2) * 128:(mc % 2) * 128 + 128], pp[:, 0:128],
                    1.0 / 64, cstf_s[:, mc:mc + 1], ALU.mult, ALU.add),
                    reads=[bpp, bcstf], writes=[bdstb])
            for s_ in range(SEQ):
                for hh in range(4):
                    fw.op("tensor", lambda t, s_=s_, hh=hh, hb=64 * (hh % 2): t.transpose(
                        pT[0:64, 0:64],
                        rvfbuf[hb:hb + 64, (hh // 2) * 128 + s_ * 64: (hh // 2) * 128 + s_ * 64 + 64],
                        cstb[hb:hb + 64, C_ID + hb:C_ID + hb + 64]),
                        reads=[brvf, bcst], writes=[bPT])
                    fw.op("vector", lambda v, s_=s_, hh=hh: v.tensor_copy(
                        rvlat[64 * s_:64 * s_ + 64, hh * 64:hh * 64 + 64], pT[0:64, 0:64]),
                        reads=[bPT], writes=[brvlat])
            sc_banks = [(pA, bPA), (pB, bPB), (pE, bPE), (pF, bPF)]
            tr_banks = [(pT, bPT), (pT2, bPT2)]
            out_banks = [(pC, bPC), (pD, bPD)]
            groups = [(s_, qc, hh) for s_ in range(SEQ) for qc in range(2) for hh in range(NH)]
            stages = []
            for gidx, (s_, qc, hh) in enumerate(groups):
                first = (i == 0 and qc == 0)
                rot = gidx % 4
                psc, bpsc = sc_banks[rot]
                ptr, bptr = tr_banks[gidx % 2]
                pout, bpout = out_banks[gidx % 2]
                pbo = pbf[:, rot * 256: rot * 256 + 256]
                ato = attnT[:, rot * 256: rot * 256 + 256]
                bpbo, bato = bpbf[rot], battnT[rot]
                mxo, deno, rdeno = mx[:, rot:rot + 1], den[:, rot:rot + 1], rden[:, rot:rot + 1]
                bmxo, bdeno, brdeno = bmx[rot], bden[rot], brden[rot]
                hch, hbase = hh // 2, 64 * (hh % 2)
                qa = qb[hbase:hbase + 64, hch * COLS + s_ * TP + qc * 128: hch * COLS + s_ * TP + qc * 128 + 128]
                W_ = 128 if first else 256
                nkc = 1 if first else 2
                trc = (rot // 2) * 512

                def S0(first=first, qa=qa, psc=psc, bpsc=bpsc, hch=hch, hbase=hbase, s_=s_, qc=qc):
                    if first:
                        fw.op("tensor", lambda t: t.matmul(
                            psc[:, 0:128], qa,
                            kb[hbase:hbase + 64, hch * COLS + s_ * TP: hch * COLS + s_ * TP + 128],
                            start=True, stop=False, skip_group_check=True),
                            reads=[blnF1[hch], bkb], writes=[bpsc], signal=False)
                        fw.op("tensor", lambda t: t.matmul(
                            psc[:, 0:128], cstb[:, C_ID:C_ID + 128], cstb[:, C_M128:C_M128 + 128],
                            start=False, stop=True, skip_group_check=True),
                            reads=[bcst], writes=[bpsc])
                    else:
                        ksrc0 = (ktail[hbase:hbase + 64, hch * SEQ * 128 + s_ * 128: hch * SEQ * 128 + s_ * 128 + 128]
                                 if qc == 0 else
                                 kb[hbase:hbase + 64, hch * COLS + s_ * TP: hch * COLS + s_ * TP + 128])
                        bks0 = bktail if qc == 0 else bkb
                        ksrc1 = kb[hbase:hbase + 64, hch * COLS + s_ * TP + qc * 128: hch * COLS + s_ * TP + qc * 128 + 128]
                        fw.op("tensor", lambda t: t.matmul(
                            psc[:, 0:128], qa, ksrc0,
                            start=True, stop=False, skip_group_check=True),
                            reads=[blnF1[hch], bks0], writes=[bpsc], signal=False)
                        fw.op("tensor", lambda t: t.matmul(
                            psc[:, 128:256], qa, ksrc1,
                            start=False, stop=False, skip_group_check=True),
                            reads=[blnF1[hch], bkb], writes=[bpsc], signal=False)
                        fw.op("tensor", lambda t: t.matmul(
                            psc[:, 0:256], cstb[:, C_ID:C_ID + 128], cstb[:, C_M256:C_M256 + 256],
                            start=False, stop=True, skip_group_check=True),
                            reads=[bcst], writes=[bpsc])

                def S1(psc=psc, bpsc=bpsc, mxo=mxo, bmxo=bmxo, W_=W_):
                    fw.op("vector", lambda v: v.reduce_max(mxo, psc[:, 0:W_], axis=AX.X, negate=True),
                          reads=[bpsc], writes=[bmxo])

                def S2(psc=psc, bpsc=bpsc, pbo=pbo, bpbo=bpbo, mxo=mxo, bmxo=bmxo, deno=deno, bdeno=bdeno, W_=W_):
                    fw.op("scalar", lambda sx: sx.activation(pbo[:, 0:W_], psc[:, 0:W_], AF.Exp, bias=mxo, accum_out=deno),
                          reads=[bpsc, bmxo], writes=[bpbo, bdeno])

                def S3(pbo=pbo, bpbo=bpbo, deno=deno, bdeno=bdeno, rdeno=rdeno, brdeno=brdeno, W_=W_):
                    fw.op("vector", lambda v: v.reciprocal(rdeno, deno), reads=[bdeno], writes=[brdeno])
                    fw.op("vector", lambda v: v.tensor_scalar(pbo[:, 0:W_], pbo[:, 0:W_], rdeno, None, ALU.mult),
                          reads=[brdeno], writes=[bpbo])

                def S4(pbo=pbo, bpbo=bpbo, ptr=ptr, bptr=bptr, trc=trc, nkc=nkc):
                    for kcb in range(nkc):
                        fw.op("tensor", lambda t, kcb=kcb: t.transpose(
                            ptr[:, trc + kcb * 128: trc + kcb * 128 + 128],
                            pbo[:, kcb * 128:kcb * 128 + 128],
                            cstb[:, C_ID:C_ID + 128]),
                            reads=[bpbo, bcst], writes=[bptr])

                def S5(ato=ato, bato=bato, ptr=ptr, bptr=bptr, trc=trc, nkc=nkc):
                    for kcb in range(nkc):
                        if kcb == 0:
                            fw.op("vector", lambda v, kcb=kcb: v.tensor_copy(
                                ato[:, kcb * 128:kcb * 128 + 128],
                                ptr[:, trc + kcb * 128: trc + kcb * 128 + 128]),
                                reads=[bptr], writes=[bato])
                        else:
                            fw.op("scalar", lambda sx, kcb=kcb: sx.activation(
                                ato[:, kcb * 128:kcb * 128 + 128],
                                ptr[:, trc + kcb * 128: trc + kcb * 128 + 128], AF.Copy),
                                reads=[bptr], writes=[bato])

                def S6(first=first, s_=s_, qc=qc, hh=hh, ato=ato, bato=bato, pout=pout, bpout=bpout, nkc=nkc):
                    for kcb in range(nkc):
                        if first:
                            vsrc, bvsrc = vb[:, (s_ * 2) * D + hh * 64:(s_ * 2) * D + hh * 64 + 64], bvb
                        elif qc == 0 and kcb == 0:
                            vsrc, bvsrc = vtail[:, s_ * D + hh * 64: s_ * D + hh * 64 + 64], bvtail
                        else:
                            tc = s_ * 2 + (qc - 1 + kcb)
                            vsrc, bvsrc = vb[:, tc * D + hh * 64: tc * D + hh * 64 + 64], bvb
                        fw.op("tensor", lambda t, vsrc=vsrc, bvsrc=bvsrc, kcb=kcb: t.matmul(
                            pout[0:64, 0:128], vsrc, ato[:, kcb * 128:kcb * 128 + 128],
                            start=(kcb == 0), stop=(kcb == nkc - 1)),
                            reads=[bvsrc, bato], writes=[bpout], signal=(kcb == nkc - 1))

                def S7(hch=hch, hbase=hbase, s_=s_, qc=qc, pout=pout, bpout=bpout, gidx=gidx):
                    dst = h2[hbase:hbase + 64, hch * COLS + s_ * TP + qc * 128: hch * COLS + s_ * TP + qc * 128 + 128]
                    if gidx % 2 == 0:
                        fw.op("scalar", lambda sx: sx.activation(dst, pout[0:64, 0:128], AF.Copy),
                              reads=[bpout], writes=[blocal[hch]])
                    else:
                        fw.op("vector", lambda v: v.tensor_copy(dst, pout[0:64, 0:128]),
                              reads=[bpout], writes=[blocal[hch]])

                stages.append([S0, S1, S2, S3, S4, S5, S6, S7])
            NS = 8
            for step in range(len(stages) + NS - 1):
                for st in range(NS - 1, -1, -1):
                    g = step - st
                    if 0 <= g < len(stages):
                        stages[g][st]()
            if i == 0:
                for s_ in range(SEQ):
                    for ch in range(DC):
                        fw.op("vector", lambda v, s_=s_, ch=ch: v.memset(
                            h2[:, ch * COLS + s_ * TP: ch * COLS + s_ * TP + 1], 0.0),
                            writes=[blocal[ch]])
            lgroups = [(s_, qc, hh) for s_ in range(SEQ) for qc in range(2) for hh in range(4)]
            lstages = []
            for gidx, (s_, qc, hh) in enumerate(lgroups):
                Cq = 4 * i + 2 * qc
                rot = gidx % 4
                psc, bpsc = sc_banks[rot]
                ptr, bptr = tr_banks[gidx % 2]
                pout, bpout = out_banks[gidx % 2]
                pbo = pbf[:, rot * 256: rot * 256 + 64]
                bpbo = bpbf[rot]
                mxo, deno, rdeno = mx[:, 4 + rot:5 + rot], den[:, 4 + rot:5 + rot], rden[:, 4 + rot:5 + rot]
                bmxo, bdeno, brdeno = bmx[4 + rot], bden[4 + rot], brden[4 + rot]
                hch, hbase = hh // 2, 64 * (hh % 2)
                rqa = rqb[hbase:hbase + 64, hch * COLS + s_ * TP + qc * 128: hch * COLS + s_ * TP + qc * 128 + 128]
                trc = (rot // 2) * 512 + 256
                lslot = (rot % 2) * 128

                def L0(rqa=rqa, psc=psc, bpsc=bpsc, hch=hch, hbase=hbase, s_=s_, Cq=Cq):
                    fw.op("tensor", lambda t: t.matmul(
                        psc[:, 0:64], rqa,
                        rkbuf[hbase:hbase + 64, hch * 128 + s_ * 64: hch * 128 + s_ * 64 + 64],
                        start=True, stop=False, skip_group_check=True),
                        reads=[brqb, brk], writes=[bpsc], signal=False)
                    fw.op("tensor", lambda t: t.matmul(
                        psc[:, 0:64], cstb[:, C_ID:C_ID + 128],
                        cstb[:, C_LMASK + 64 - Cq: C_LMASK + 128 - Cq],
                        start=False, stop=True, skip_group_check=True),
                        reads=[bcst], writes=[bpsc])

                def L1(psc=psc, bpsc=bpsc, mxo=mxo, bmxo=bmxo):
                    fw.op("vector", lambda v: v.reduce_max(mxo, psc[:, 0:64], axis=AX.X, negate=True),
                          reads=[bpsc], writes=[bmxo])

                def L2(psc=psc, bpsc=bpsc, pbo=pbo, bpbo=bpbo, mxo=mxo, bmxo=bmxo, deno=deno, bdeno=bdeno):
                    fw.op("scalar", lambda sx: sx.activation(pbo, psc[:, 0:64], AF.Exp, bias=mxo, accum_out=deno),
                          reads=[bpsc, bmxo], writes=[bpbo, bdeno])

                def L3(pbo=pbo, bpbo=bpbo, deno=deno, bdeno=bdeno, rdeno=rdeno, brdeno=brdeno, qc=qc):
                    fw.op("vector", lambda v: v.reciprocal(rdeno, deno), reads=[bdeno], writes=[brdeno])
                    fw.op("vector", lambda v: v.tensor_scalar(pbo, pbo, rdeno, None, ALU.mult),
                          reads=[brdeno], writes=[bpbo])
                    if i == 0 and qc == 0:
                        fw.op("vector", lambda v: v.memset(pbo[0:64, :], 0.0), writes=[bpbo])

                def L4(pbo=pbo, bpbo=bpbo, ptr=ptr, bptr=bptr, trc=trc):
                    fw.op("tensor", lambda t: t.transpose(
                        ptr[0:64, trc: trc + 128], pbo, cstb[:, C_ID:C_ID + 128]),
                        reads=[bpbo, bcst], writes=[bptr])

                def L5(s_=s_, ptr=ptr, bptr=bptr, trc=trc, lslot=lslot):
                    fw.op("vector", lambda v: v.tensor_copy(
                        lattnT[64 * s_:64 * s_ + 64, lslot:lslot + 128],
                        ptr[0:64, trc: trc + 128]),
                        reads=[bptr], writes=[blattnT])

                def L6(s_=s_, hh=hh, lslot=lslot, pout=pout, bpout=bpout):
                    fw.op("tensor", lambda t: t.matmul(
                        pout[0:64, 128:256],
                        rvlat[64 * s_:64 * s_ + 64, hh * 64:hh * 64 + 64],
                        lattnT[64 * s_:64 * s_ + 64, lslot:lslot + 128],
                        start=True, stop=True, skip_group_check=True),
                        reads=[brvlat, blattnT], writes=[bpout])

                def L7(hch=hch, hbase=hbase, s_=s_, qc=qc, pout=pout, bpout=bpout):
                    fw.op("scalar", lambda sx: sx.activation(
                        latout[hbase:hbase + 64, hch * COLS + s_ * TP + qc * 128: hch * COLS + s_ * TP + qc * 128 + 128],
                        pout[0:64, 128:256], AF.Copy),
                        reads=[bpout], writes=[blatout])

                lstages.append([L0, L1, L2, L3, L4, L5, L6, L7])
            for step in range(len(lstages) + NS - 1):
                for st in range(NS - 1, -1, -1):
                    g = step - st
                    if 0 <= g < len(lstages):
                        lstages[g][st]()
            sl, bs, _ = stream("rout", 0, 2048)
            for mc in range(DC):
                pp, bpp = (pA, bPA) if mc % 2 == 0 else (pB, bPB)
                for kc in range(2):
                    fw.op("tensor", lambda t, sl=sl, mc=mc, kc=kc, pp=pp: t.matmul(
                        pp[:, 0:COLS], sl[:, mc * LD + kc * 128: mc * LD + kc * 128 + 128],
                        latout[:, kc * COLS:(kc + 1) * COLS],
                        start=(kc == 0), stop=False),
                        reads=[bs, blatout], writes=[bpp], signal=False)
                fw.op("tensor", lambda t, mc=mc, pp=pp: t.matmul(
                    pp[:, 0:COLS], bias_bf[0:1, B_OFFS["rout"] + mc * 128: B_OFFS["rout"] + mc * 128 + 128],
                    ones1b[0:1, 0:COLS], start=False, stop=True),
                    reads=[bbias, bones1b], writes=[bpp])
                fw.op("scalar", lambda s, mc=mc, pp=pp: s.activation(
                    h2[:, (DC + mc) * COLS:(DC + mc + 1) * COLS], pp[:, 0:COLS], AF.Copy),
                    reads=[bpp], writes=[bremote[mc]])
            if mid is not None:
                mid()
            gstages = []
            for ch in range(DC):
                pg_, bpg_ = (pE, bPE) if ch % 2 == 0 else (pF, bPF)
                pb_, bpb_ = (pC, bPC) if ch % 2 == 0 else (pD, bPD)
                gsl = ch % 2
                sc2, bsc2 = scr[ch % 2], bscr[ch % 2]

                def G0(ch=ch, pg_=pg_, bpg_=bpg_):
                    fw.op("tensor", lambda t: t.matmul(
                        pg_[0:2, 0:COLS], cstb[:, C_GL:C_GL + 2], h2[:, ch * COLS:(ch + 1) * COLS],
                        start=True, stop=False),
                        reads=[bcst, blocal[ch]], writes=[bpg_], signal=False)
                    fw.op("tensor", lambda t: t.matmul(
                        pg_[0:2, 0:COLS], cstb[:, C_GR:C_GR + 2], h2[:, (DC + ch) * COLS:(DC + ch + 1) * COLS],
                        start=False, stop=True),
                        reads=[bcst, bremote[ch]], writes=[bpg_])

                def G1(pg_=pg_, bpg_=bpg_, gsl=gsl):
                    fw.op("scalar", lambda sx: sx.activation(gatebf[:, gsl * COLS:(gsl + 1) * COLS], pg_[0:2, 0:COLS], AF.Sigmoid),
                          reads=[bpg_], writes=[bgate])

                def G2(pb_=pb_, bpb_=bpb_, gsl=gsl):
                    fw.op("tensor", lambda t: t.matmul(
                        pb_[:, 0:COLS], cstb[0:2, C_SEL2:C_SEL2 + 128], gatebf[:, gsl * COLS:(gsl + 1) * COLS],
                        start=True, stop=True),
                        reads=[bcst, bgate], writes=[bpb_])

                def G3(ch=ch, sc2=sc2, bsc2=bsc2):
                    fw.op("vector", lambda v: v.tensor_sub(
                        sc2[:], h2[:, ch * COLS:(ch + 1) * COLS], h2[:, (DC + ch) * COLS:(DC + ch + 1) * COLS]),
                        reads=[blocal[ch], bremote[ch]], writes=[bsc2])

                def G4(sc2=sc2, bsc2=bsc2, pb_=pb_, bpb_=bpb_):
                    fw.op("vector", lambda v: v.tensor_mul(sc2[:], sc2[:], pb_[:, 0:COLS]),
                          reads=[bsc2, bpb_], writes=[bsc2])

                def G5(ch=ch, sc2=sc2, bsc2=bsc2):
                    fw.op("vector", lambda v: v.tensor_add(
                        lnB[:, ch * COLS:(ch + 1) * COLS], sc2[:], h2[:, (DC + ch) * COLS:(DC + ch + 1) * COLS]),
                        reads=[bsc2, bremote[ch]], writes=[blnB[ch]])

                gstages.append([G0, G1, G2, G3, G4, G5])
            for step in range(len(gstages) + 5):
                for st in range(5, -1, -1):
                    g_ = step - st
                    if 0 <= g_ < len(gstages):
                        gstages[g_][st]()
            if mid2 is not None:
                mid2()
            yb, byb = y[i % NYB], bY[i % NYB]
            for half in range(2):
                sl, bs, _ = stream("outw", half * 4096, (half + 1) * 4096)
                for mcl in range(4):
                    mc = half * 4 + mcl
                    pp, bpp = (pC, bPC) if mc % 2 == 0 else (pD, bPD)
                    for kc in range(DC):
                        fw.op("tensor", lambda t, sl=sl, mcl=mcl, kc=kc, pp=pp: t.matmul(
                            pp[:, 0:COLS], sl[:, mcl * D + kc * 128: mcl * D + kc * 128 + 128],
                            lnB[:, kc * COLS:(kc + 1) * COLS],
                            start=(kc == 0), stop=False),
                            reads=[bs, blnB[kc]], writes=[bpp], signal=False)
                    fw.op("tensor", lambda t, mc=mc, pp=pp: t.matmul(
                        pp[:, 0:COLS], bias_bf[0:1, B_OFFS["outw"] + mc * 128: B_OFFS["outw"] + mc * 128 + 128],
                        ones1b[0:1, 0:COLS], start=False, stop=True),
                        reads=[bbias, bones1b], writes=[bpp])
                    fw.op("vector", lambda v, mc=mc, pp=pp: v.tensor_add(
                        yb[:, mc * COLS:(mc + 1) * COLS], yb[:, mc * COLS:(mc + 1) * COLS], pp[:, 0:COLS]),
                        reads=[bpp, byb[mc]], writes=[byb[mc]])
                    if post_chunk is not None:
                        post_chunk(mc)
            fw.op("gpsimd", lambda g: g.tensor_copy(
                ktail.ap().rearrange("p (c s k) -> p c s k", c=DC, s=SEQ),
                kb.ap().rearrange("p (c s pp) -> p c s pp", c=DC, s=SEQ)[:, :, :, 128:256]),
                reads=[bkb], writes=[bktail])
            fw.op("gpsimd", lambda g: g.tensor_copy(
                vtail.ap().rearrange("p (s d) -> p s d", s=SEQ),
                vb.ap().rearrange("p (s two d) -> p s two d", s=SEQ, two=2)[:, :, 1]),
                reads=[bvb], writes=[bvtail])

        bqb_l = blnF1
        bkb_l = [bkb] * DC

        def mk_passA(i):
            yb, byb = y[i % NYB], bY[i % NYB]
            off = i * DC * COLS

            def store(ch, yb=yb, byb=byb, off=off):
                fw.op("gpd", lambda g, ch=ch: g.dma_start(
                    out=y_d[:, off + ch * COLS: off + (ch + 1) * COLS],
                    in_=yb[:, ch * COLS:(ch + 1) * COLS]),
                    reads=[byb[ch]], writes=[bydram], dma=True)
            return {"src": lnA, "bsrc": blnA, "yb": yb, "byb": byb,
                    "h": h, "bh": bh, "rj": 0, "post_chunk": store,
                    "pre": lambda i=i: ln_p2(i, lnA, blnA)}

        def mk_passB(i):
            return {"src": lnF1, "bsrc": blnF1, "yb": y[i % NYB], "byb": bY[i % NYB],
                    "h": h2, "bh": bh2, "rj": 1,
                    "post_chunk": lambda ch, i=i: ln_p1(i, lnB, blnB, ch)}

        load_x(0)
        ln_pass(0, lnF1, blnF1)
        ffn_multi([mk_passB(0)])
        for i in range(NT):
            if i + 1 < NT:
                load_x(i + 1)

            def mid(i=i):
                if i + 1 < NT:
                    for ch in range(DC):
                        ln_p1(i + 1, lnF1, blnF1, ch)

            def mid2(i=i):
                if i + 1 < NT:
                    ln_p2(i + 1, lnF1, blnF1)
            attn(i, pre=lambda i=i: ln_p2(i, lnB, blnB),
                 post_chunk=lambda ch, i=i: ln_p1(i, lnA, blnA, ch),
                 mid=mid, mid2=mid2)
            if i == 0:
                slot_state["f32"] = False
            plist = []
            if i + 1 < NT:
                plist.append(mk_passB(i + 1))
            plist.append(mk_passA(i))
            ffn_multi(plist)

        fw.finalize(stack)
    return nc


def kernel(**inputs):
    in_maps = _host_layouts(inputs)
    nc = build_program(_PHASES)
    res = run_bass_kernel_spmd(nc, in_maps, core_ids=list(range(NCORES)))
    return _unshard_output(res)


# version 25
# speedup vs baseline: 1.1928x; 1.0838x over previous
import contextlib
import numpy as np
import ml_dtypes
import concourse.bass as bass
import concourse.mybir as mybir
from concourse.bass_utils import run_bass_kernel_spmd

F32 = mybir.dt.float32
BF16 = mybir.dt.bfloat16
AF = mybir.ActivationFunctionType
ALU = mybir.AluOpType
AX = mybir.AxisListType

NCORES = 8
SEQ = 2
S = 4096
D = 1024
DC = 8
I = 2048
IC = 16
NE = 8
NH = 16
HD = 64
TP = 256
NT = S // TP
COLS = SEQ * TP
NLAT = 64
LD = 256
NEG = -1e9

_PHASES = ("ffn1", "attn", "ffn2")

ENGINES = ("tensor", "vector", "scalar", "gpsimd", "gpd", "sync")


class Buf:
    __slots__ = ("name", "writer", "readers")

    def __init__(self, name):
        self.name = name
        self.writer = None
        self.readers = []


class FW:

    def __init__(self, nc):
        self.nc = nc
        self.q = {e: [] for e in ENGINES}
        self.cnt = {e: 0 for e in ENGINES}
        self.hw = {e: {f: 0 for f in ENGINES} for e in ENGINES}
        self.sems = {}
        self.n_waits = 0

    def buf(self, name):
        return Buf(name)

    def op(self, eng, fn, reads=(), writes=(), deps=(), dma=False, signal=True):
        inc = 16 if dma else 1
        need = list(deps)
        for b in reads:
            if b.writer is not None:
                need.append(b.writer)
        for b in writes:
            if b.writer is not None:
                need.append(b.writer)
            need.extend(b.readers)
        per = {}
        for (e2, v) in need:
            assert v <= self.cnt[e2] + (0 if e2 != eng else 1), f"future dep {e2}:{v}"
            if e2 == eng and v > self.cnt[e2]:
                continue
            if per.get(e2, 0) < v:
                per[e2] = v
        waits = []
        for e2 in sorted(per):
            v = per[e2]
            if self.hw[eng][e2] >= v:
                continue
            self.hw[eng][e2] = v
            waits.append((e2, v))
        self.n_waits += len(waits)
        if signal:
            tok = (eng, self.cnt[eng] + inc)
            self.cnt[eng] += inc
        else:
            tok = (eng, self.cnt[eng] + 1)

        def closure(engobj, fw=self, waits=tuple(waits), fn=fn, inc=inc, eng=eng, signal=signal):
            for (e2, v) in waits:
                engobj.wait_ge(fw.sems[e2], v)
            ins = fn(engobj)
            if signal:
                ins.then_inc(fw.sems[eng], inc)

        self.q["gpsimd" if eng == "gpd" else eng].append(closure)
        for b in reads:
            b.readers.append(tok)
        for b in writes:
            b.writer = tok
            b.readers = []
        return tok

    def prewait(self, eng, reads=(), writes=(), deps=()):
        need = list(deps)
        for b in reads:
            if b.writer is not None:
                need.append(b.writer)
        for b in writes:
            if b.writer is not None:
                need.append(b.writer)
            need.extend(b.readers)
        per = {}
        for (e2, v) in need:
            if e2 == eng and v > self.cnt[e2]:
                continue
            if per.get(e2, 0) < v:
                per[e2] = v
        waits = []
        for e2 in sorted(per):
            v = per[e2]
            if self.hw[eng][e2] >= v:
                continue
            self.hw[eng][e2] = v
            waits.append((e2, v))
        if not waits:
            return
        self.n_waits += len(waits)

        def closure(engobj, fw=self, waits=tuple(waits)):
            for (e2, v) in waits:
                engobj.wait_ge(fw.sems[e2], v)

        self.q["gpsimd" if eng == "gpd" else eng].append(closure)

    def finalize(self, stack):
        nc = self.nc
        for e in ENGINES:
            self.sems[e] = stack.enter_context(nc.semaphore(f"sem_{e}"))
        block = stack.enter_context(nc.Block())
        fw = self

        def emit(engobj, e):
            for c in fw.q[e]:
                c(engobj)
        assert not fw.q["gpd"]

        @block.tensor
        def _(t):
            emit(t, "tensor")

        @block.vector
        def _(v):
            emit(v, "vector")

        @block.scalar
        def _(s):
            emit(s, "scalar")

        @block.gpsimd
        def _(g):
            emit(g, "gpsimd")
            for e in ENGINES:
                if fw.cnt[e] > 0:
                    g.wait_ge(fw.sems[e], fw.cnt[e])

        @block.sync
        def _(s):
            emit(s, "sync")


_W_SPECS = [
    ("qkv", D, 3 * D),
    ("rq", D, LD),
    ("rkv", D, 2 * LD),
    ("rout", LD, D),
    ("outw", D, D),
    ("wr", D, 8),
    ("gs", D, I), ("us", D, I), ("ds", I, D),
]
for _e in range(NE):
    _W_SPECS += [(f"g{_e}", D, I), (f"u{_e}", D, I), (f"d{_e}", I, D)]


def _w_offsets():
    offs, off = {}, 0
    for name, K, N in _W_SPECS:
        offs[name] = (off, K, N)
        off += (K // 128) * N
    return offs, off


W_OFFS, W_COLS = _w_offsets()

B_OFFS = {"qkv": 0, "rq": 3072, "rkv": 3328, "rout": 3840, "outw": 4864, "rb": 5888}
B_COLS = 5896

C_M256, C_M128, C_LMASK, C_ID, C_GL, C_GR, C_SEL2, C_SELE = 0, 256, 384, 512, 640, 642, 644, 772
C_COLS = 772 + 8 * 128
CF_COLS = 8


def _host_layouts(ins):
    x = np.ascontiguousarray(ins["L0_attn_x"])

    mats = {
        "qkv": np.ascontiguousarray(ins["L0_attn_qkv_w"]),
        "rq": np.ascontiguousarray(ins["L0_attn_rq_w"]),
        "rkv": np.ascontiguousarray(ins["L0_attn_rkv_w"]),
        "rout": np.ascontiguousarray(ins["L0_attn_rout_w"]),
        "outw": np.ascontiguousarray(ins["L0_attn_out_w"]),
        "wr": np.ascontiguousarray(ins["L0_ffn_Wr"].T),
        "gs": np.ascontiguousarray(ins["L0_ffn_Wg_s"].T),
        "us": np.ascontiguousarray(ins["L0_ffn_Wu_s"].T),
        "ds": np.ascontiguousarray(ins["L0_ffn_Wd_s"].T),
    }
    for e in range(NE):
        mats[f"g{e}"] = np.ascontiguousarray(ins["L0_ffn_Wg"][e].T)
        mats[f"u{e}"] = np.ascontiguousarray(ins["L0_ffn_Wu"][e].T)
        mats[f"d{e}"] = np.ascontiguousarray(ins["L0_ffn_Wd"][e].T)

    wf = np.empty((128, W_COLS), np.float32)
    for name, (off, K, N) in W_OFFS.items():
        W = mats[name]
        KC = K // 128
        if N >= 128:
            MC = N // 128
            arr = W.reshape(KC, 128, MC, 128).transpose(1, 2, 0, 3).reshape(128, KC * N)
        else:
            arr = W.reshape(KC, 128, N).transpose(1, 0, 2).reshape(128, KC * N)
        wf[:, off:off + KC * N] = arr

    bia = np.zeros((1, B_COLS), np.float32)
    bia[0, 0:3072] = ins["L0_attn_qkv_b"]
    bia[0, 3072:3328] = ins["L0_attn_rq_b"]
    bia[0, 3328:3840] = ins["L0_attn_rkv_b"]
    bia[0, 3840:4864] = ins["L0_attn_rout_b"]
    bia[0, 4864:5888] = ins["L0_attn_out_b"]
    bia[0, 5888:5896] = ins["L0_ffn_rb"]

    cst = np.zeros((128, C_COLS), np.float32)
    i_ = np.arange(128)[:, None]
    j_ = np.arange(256)[None, :]
    cst[:, C_M256:C_M256 + 256] = np.where((j_ >= i_) & (j_ <= i_ + 127), 0.0, NEG)
    j2 = np.arange(128)[None, :]
    cst[:, C_M128:C_M128 + 128] = np.where(j2 <= i_ - 1, 0.0, NEG)
    lm = np.where(j2 <= 63, 0.0, NEG)
    lm = np.where((i_ >= 64) & (j2 == 64), 0.0, lm)
    cst[:, C_LMASK:C_LMASK + 128] = lm
    cst[:, C_ID:C_ID + 128] = np.eye(128, dtype=np.float32)
    gl = ins["L0_attn_gl_w"].reshape(64)
    gr = ins["L0_attn_gr_w"].reshape(64)
    cst[0:64, C_GL] = gl
    cst[64:128, C_GL + 1] = gl
    cst[0:64, C_GR] = gr
    cst[64:128, C_GR + 1] = gr
    cst[0, C_SEL2:C_SEL2 + 64] = 1.0
    cst[1, C_SEL2 + 64:C_SEL2 + 128] = 1.0
    for e in range(NE):
        cst[e, C_SELE + e * 128:C_SELE + (e + 1) * 128] = 1.0

    cstf = np.zeros((128, CF_COLS), np.float32)
    cstf[:, 0:4] = ins["L0_attn_rkv_b"].reshape(4, 128).T
    cstf[0, 4] = 1e-5

    in_maps = []
    for c in range(NCORES):
        xc = x[SEQ * c:SEQ * (c + 1)]
        arr = xc.reshape(SEQ, NT, TP, DC, 128).transpose(4, 1, 3, 0, 2).reshape(128, NT * DC * COLS)
        in_maps.append({
            "x": np.ascontiguousarray(arr),
            "wf": wf, "bia": bia, "cst": cst, "cstf": cstf,
        })
    return in_maps


def _unshard_output(res):
    outs = []
    for c in range(NCORES):
        y = np.asarray(res.results[c]["y"], np.float32)
        a = y.reshape(128, NT, DC, SEQ, TP).transpose(3, 1, 4, 2, 0).reshape(SEQ, S, D)
        outs.append(a)
    return np.concatenate(outs, 0)


def build_program(phases=None):
    if phases is None:
        phases = _PHASES
    nc = bass.Bass("TRN2", target_bir_lowering=False)
    x_d = nc.dram_tensor("x", [128, NT * DC * COLS], F32, kind="ExternalInput").ap()
    wf_d = nc.dram_tensor("wf", [128, W_COLS], F32, kind="ExternalInput").ap()
    bia_d = nc.dram_tensor("bia", [1, B_COLS], F32, kind="ExternalInput").ap()
    cst_d = nc.dram_tensor("cst", [128, C_COLS], F32, kind="ExternalInput").ap()
    cstf_d = nc.dram_tensor("cstf", [128, CF_COLS], F32, kind="ExternalInput").ap()
    y_d = nc.dram_tensor("y", [128, NT * DC * COLS], F32, kind="ExternalOutput").ap()
    wbf_d = nc.dram_tensor("wbf", [128, W_COLS], BF16, kind="Internal").ap()

    stack = contextlib.ExitStack()
    with stack:
        SB = lambda n, sh, dt: stack.enter_context(nc.sbuf_tensor(n, sh, dt))
        PS = lambda n, sh, dt: stack.enter_context(nc.psum_tensor(n, sh, dt))

        NYB = 2
        NWS = 4
        y = [SB(f"y{j}", [128, DC * COLS], F32) for j in range(NYB)]
        lnA = SB("lnA", [128, DC * COLS], BF16)
        lnB = SB("lnB", [128, DC * COLS], BF16)
        h = SB("h", [128, IC * COLS], BF16)
        h2 = SB("h2", [128, IC * COLS], BF16)
        cwb = SB("cwb", [128, 4 * COLS], BF16)
        sg = [SB(f"sg{j}", [128, COLS], F32) for j in range(2)]
        sqf = SB("sqf", [128, 2 * COLS], BF16)
        scr = [SB(f"scr{j}", [128, COLS], F32) for j in range(2)]
        gscr = [SB(f"gscr{j}", [128, COLS], F32) for j in range(2)]
        wring = [SB(f"w{i}", [128, 4096], BF16) for i in range(NWS)]
        qb = SB("qb", [128, DC * COLS], BF16)
        lnF1 = qb
        kb = SB("kb", [128, DC * COLS], BF16)
        ktail = SB("ktail", [128, DC * SEQ * 128], BF16)
        vb = SB("vb", [128, 4 * D], BF16)
        vtail = SB("vtail", [128, SEQ * D], BF16)
        rqb = SB("rqb", [128, 2 * COLS], BF16)
        pooled = SB("pooled", [128, DC * SEQ * NLAT], BF16)
        rkbuf = SB("rkbuf", [128, 2 * 128], BF16)
        rvfbuf = SB("rvfbuf", [128, 2 * 128], BF16)
        rvlat = SB("rvlat", [128, LD], BF16)
        latout = SB("latout", [128, 2 * COLS], BF16)
        pbf = SB("pbf", [128, 4 * 256], BF16)
        attnT = SB("attnT", [128, 4 * 256], BF16)
        lattnT = SB("lattnT", [128, 2 * 128], BF16)
        mx = SB("mx", [128, 8], F32)
        den = SB("den", [128, 8], F32)
        rden = SB("rden", [128, 8], F32)
        gatebf = SB("gatebf", [2, 2 * COLS], BF16)
        stats = SB("stats", [1, 4 * COLS], F32)
        probs = [SB(f"probs{j}", [128, 4 * NE], F32) for j in range(2)]
        top2 = [SB(f"top2{j}", [128, 64], F32) for j in range(2)]
        cwtok = [SB(f"cwtok{j}", [128, 4 * NE], F32) for j in range(2)]
        cwtb = [SB(f"cwtb{j}", [128, 4 * NE], BF16) for j in range(2)]
        cwf = [SB(f"cwf{j}", [8, COLS], BF16) for j in range(2)]
        bias_bf = SB("bias_bf", [1, B_COLS], BF16)
        cstb = SB("cstb", [128, C_COLS], BF16)
        cstf_s = SB("cstf_s", [128, CF_COLS], F32)
        ones1b = SB("ones1b", [1, 512], BF16)
        ones128b = SB("ones128b", [128, 1], BF16)
        ones128f = SB("ones128f", [128, 1], F32)
        ones1f = SB("ones1f", [1, 128], F32)
        wrres = SB("wrres", [128, 64], BF16)

        pA = PS("pA", [128, 512], F32)
        pB = PS("pB", [128, 512], F32)
        pC = PS("pC", [128, 512], F32)
        pD = PS("pD", [128, 512], F32)
        pE = PS("pE", [128, 512], F32)
        pF = PS("pF", [128, 512], F32)
        pT = PS("pT", [128, 1024], BF16)
        pT2 = PS("pT2", [128, 1024], BF16)

        fw = FW(nc)
        bufs = {}

        def B(name):
            if name not in bufs:
                bufs[name] = fw.buf(name)
            return bufs[name]

        def BL(name, n):
            return [B(f"{name}.{j}") for j in range(n)]

        bY = [BL(f"y{j}", DC) for j in range(NYB)]
        blnA = BL("lnA", DC)
        blnB = BL("lnB", DC)
        blnF1 = BL("lnF1", DC)
        bh = BL("h", IC)
        bh2 = BL("h2", IC)
        bcwb = [B(f"cwb{j}") for j in range(4)]
        bsg = [B(f"sg{j}") for j in range(2)]
        bsqf = BL("sqf", 2)
        bscr = [B(f"scr{j}") for j in range(2)]
        bgscr = [B(f"gscr{j}") for j in range(2)]
        bwr = [B(f"w{i}") for i in range(NWS)]
        bkb, bktail, bvb, bvtail = B("kb"), B("ktail"), B("vb"), B("vtail")
        brqb, bpooled, brk, brvf, brvlat = B("rqb"), B("pooled"), B("rkbuf"), B("rvfbuf"), B("rvlat")
        blatout = B("latout")
        blocal = bh2[0:DC]
        bremote = bh2[DC:]
        bpbf = BL("pbf", 4)
        battnT = BL("attnT", 4)
        blattnT = B("lattnT")
        bmx = BL("mx", 8)
        bden = BL("den", 8)
        brden = BL("rden", 8)
        bgate, bstats = B("gatebf"), B("stats")
        bprobs = [B(f"probs{j}") for j in range(2)]
        btop2 = [B(f"top2{j}") for j in range(2)]
        bcwtok = [B(f"cwtok{j}") for j in range(2)]
        bcwtb = [B(f"cwtb{j}") for j in range(2)]
        bcwf = [B(f"cwf{j}") for j in range(2)]
        bbias, bcst, bcstf = B("bias_bf"), B("cstb"), B("cstf_s")
        bones1b, bones128b, bones1f = B("ones1b"), B("ones128b"), B("ones1f")
        bones128f = B("ones128f")
        bwrres = B("wrres")
        bPA, bPB, bPC, bPD, bPE, bPF, bPT, bPT2 = (B(n) for n in
            ("pA", "pB", "pC", "pD", "pE", "pF", "pT", "pT2"))
        NWCH = (W_COLS + 4095) // 4096
        bwbf = BL("wbf", NWCH)
        bydram = B("ydram")

        fw.op("gpd", lambda g: g.dma_start(out=bias_bf[:], in_=bia_d), writes=[bbias], dma=True)
        fw.op("gpd", lambda g: g.dma_start(out=cstb[:], in_=cst_d), writes=[bcst], dma=True)
        fw.op("gpd", lambda g: g.dma_start(out=cstf_s[:], in_=cstf_d), writes=[bcstf], dma=True)
        fw.op("gpsimd", lambda g: g.memset(ones1b[:], 1.0), writes=[bones1b])
        fw.op("gpsimd", lambda g: g.memset(ones128b[:], 1.0), writes=[bones128b])
        fw.op("gpsimd", lambda g: g.memset(ones128f[:], 1.0), writes=[bones128f])
        fw.op("gpsimd", lambda g: g.memset(ones1f[:], 1.0), writes=[bones1f])
        fw.op("gpsimd", lambda g: g.memset(pooled[:], 0.0), writes=[bpooled])
        fw.op("gpsimd", lambda g: g.memset(rkbuf[:], 0.0), writes=[brk])
        fw.op("gpsimd", lambda g: g.memset(rvfbuf[:], 0.0), writes=[brvf])
        fw.op("gpsimd", lambda g: g.memset(rvlat[:], 0.0), writes=[brvlat])

        def load_x(i):
            yb, byb = y[i % NYB], bY[i % NYB]
            off = i * DC * COLS
            fw.op("gpd", lambda g, yb=yb, off=off: g.dma_start(out=yb[:], in_=x_d[:, off:off + DC * COLS]),
                  writes=byb, dma=True)

        woff = W_OFFS["wr"][0]
        fw.op("gpd", lambda g: g.dma_start(out=wrres[:], in_=wf_d[:, woff:woff + 64]),
              writes=[bwrres], dma=True)
        load_x(0)

        slot_state = {"i": 0, "f32": True}

        def stream(name, cols_a, cols_b):
            off, K, N = W_OFFS[name]
            si = slot_state["i"] % NWS
            slot_state["i"] += 1
            sl, bs = wring[si], bwr[si]
            a, b = off + cols_a, off + cols_b
            rng = list(range(a // 4096, (b - 1) // 4096 + 1))
            if slot_state["f32"]:
                fw.op("gpd", lambda g, sl=sl, a=a, b=b: g.dma_start(out=sl[:, 0:b - a], in_=wf_d[:, a:b]),
                      writes=[bs], dma=True)
                fw.op("sync", lambda s, sl=sl, a=a, b=b: s.dma_start(out=wbf_d[:, a:b], in_=sl[:, 0:b - a]),
                      reads=[bs], writes=[bwbf[j] for j in rng], dma=True)
                return sl, bs, cols_a
            fw.op("sync", lambda s, sl=sl, a=a, b=b: s.dma_start(out=sl[:, 0:b - a], in_=wbf_d[:, a:b]),
                  reads=[bwbf[j] for j in rng], writes=[bs], dma=True)
            return sl, bs, cols_a

        def CC(ch, s=0, p0=0):
            return ch * COLS + s * TP + p0

        def ln_p1(i, dst, bdst, ch):
            yb, byb = y[i % NYB], bY[i % NYB]
            fw.op("vector", lambda v, ch=ch: v.tensor_copy(dst[:, CC(ch):CC(ch) + COLS], yb[:, CC(ch):CC(ch) + COLS]),
                  reads=[byb[ch]], writes=[bdst[ch]])
            fw.op("tensor", lambda t, ch=ch: t.matmul(pA[0:1, 0:COLS], ones128b[:], dst[:, CC(ch):CC(ch) + COLS],
                                                      start=(ch == 0), stop=(ch == DC - 1), skip_group_check=True),
                  reads=[bones128b, bdst[ch]], writes=[bPA])
            sqs = (ch % 2) * COLS
            fw.op("scalar", lambda sx, ch=ch, sqs=sqs: sx.activation(sqf[:, sqs:sqs + COLS], yb[:, CC(ch):CC(ch) + COLS], AF.Square),
                  reads=[byb[ch]], writes=[bsqf[ch % 2]])
            fw.op("tensor", lambda t, ch=ch, sqs=sqs: t.matmul(pB[0:1, 0:COLS], ones128b[:], sqf[:, sqs:sqs + COLS],
                                                      start=(ch == 0), stop=(ch == DC - 1), skip_group_check=True),
                  reads=[bones128b, bsqf[ch % 2]], writes=[bPB])

        def spacer(tok):
            fw.op("tensor", lambda t: t.matmul(
                pB[0:1, 0:128], ones128b[0:1, 0:1], ones1b[0:1, 0:128],
                start=True, stop=True, skip_group_check=True),
                reads=[bones128b, bones1b], writes=[bPB], deps=[tok])

        def ln_p2(i, dst, bdst):
            yb, byb = y[i % NYB], bY[i % NYB]
            m_ = lambda: stats[0:1, 0:COLS]
            m2_ = lambda: stats[0:1, COLS:2 * COLS]
            sqm_ = lambda: stats[0:1, 2 * COLS:3 * COLS]
            rstd_ = lambda: stats[0:1, 3 * COLS:4 * COLS]
            mrs_ = m2_
            var_ = m2_
            fw.op("scalar", lambda sx: sx.activation(m_(), pA[0:1, 0:COLS], AF.Copy, scale=1.0 / D),
                  reads=[bPA], writes=[bstats])
            tk = fw.op("scalar", lambda sx: sx.activation(sqm_(), pB[0:1, 0:COLS], AF.Copy, scale=1.0 / D),
                  reads=[bPB], writes=[bstats])
            spacer(tk)
            tk = fw.op("scalar", lambda sx: sx.activation(m2_(), m_(), AF.Square),
                  reads=[bstats], writes=[bstats])
            spacer(tk)
            tk = fw.op("vector", lambda v: v.tensor_sub(var_(), sqm_(), m2_()),
                  reads=[bstats], writes=[bstats])
            spacer(tk)
            tk = fw.op("scalar", lambda sx: sx.activation(sqm_(), var_(), AF.Ln, bias=cstf_s[0:1, 4:5]),
                  reads=[bstats, bcstf], writes=[bstats])
            spacer(tk)
            tk = fw.op("scalar", lambda sx: sx.activation(rstd_(), sqm_(), AF.Exp, scale=-0.5),
                  reads=[bstats], writes=[bstats])
            spacer(tk)
            fw.op("vector", lambda v: v.tensor_mul(mrs_(), m_(), rstd_()),
                  reads=[bstats], writes=[bstats])
            fw.op("tensor", lambda t: t.matmul(pE[:, 0:COLS], ones1f[0:1, 0:128], rstd_(), start=True, stop=True),
                  reads=[bones1f, bstats], writes=[bPE])
            fw.op("tensor", lambda t: t.matmul(pF[:, 0:COLS], ones1f[0:1, 0:128], mrs_(), start=True, stop=True),
                  reads=[bones1f, bstats], writes=[bPF])
            fw.op("vector", lambda v: v.tensor_copy(gscr[0][:], pE[:, 0:COLS]),
                  reads=[bPE], writes=[bgscr[0]])
            fw.op("vector", lambda v: v.tensor_copy(gscr[1][:], pF[:, 0:COLS]),
                  reads=[bPF], writes=[bgscr[1]])
            for ch in range(DC):
                sc, bsc = scr[ch % 2], bscr[ch % 2]
                fw.op("vector", lambda v, ch=ch, sc=sc: v.tensor_mul(sc[:], yb[:, CC(ch):CC(ch) + COLS], gscr[0][:]),
                      reads=[byb[ch], bgscr[0]], writes=[bsc])
                tk = fw.op("vector", lambda v, ch=ch, sc=sc: v.tensor_sub(dst[:, CC(ch):CC(ch) + COLS], sc[:], gscr[1][:]),
                      reads=[bsc, bgscr[1]], writes=[bdst[ch]])
                if ch % 2 == 1:
                    spacer(tk)

        def ln_pass(i, dst, bdst):
            for ch in range(DC):
                ln_p1(i, dst, bdst, ch)
            ln_p2(i, dst, bdst)

        def routing(src, bsrc, rj):
            prb, tpb, ctk, ctb, cwfb = probs[rj], top2[rj], cwtok[rj], cwtb[rj], cwf[rj]
            bprb, btpb, bctk, bctb, bcwfb = bprobs[rj], btop2[rj], bcwtok[rj], bcwtb[rj], bcwf[rj]
            for kc in range(DC):
                for tc in range(4):
                    fw.op("tensor", lambda t, tc=tc, kc=kc: t.matmul(
                        pD[0:128, tc * 8:tc * 8 + 8],
                        src[:, kc * COLS + tc * 128: kc * COLS + tc * 128 + 128],
                        wrres[:, kc * 8:kc * 8 + 8],
                        start=(kc == 0 and tc == 0), stop=False, skip_group_check=True),
                        reads=[bsrc[kc], bwrres], writes=[bPD], signal=False)
            for tc in range(4):
                fw.op("tensor", lambda t, tc=tc: t.matmul(
                    pD[0:128, tc * 8:tc * 8 + 8], ones1b[0:1, 0:128],
                    bias_bf[0:1, B_OFFS["rb"]:B_OFFS["rb"] + 8],
                    start=False, stop=(tc == 3), skip_group_check=True),
                    reads=[bones1b, bbias], writes=[bPD], signal=(tc == 3))
            fw.op("scalar", lambda s: s.activation(prb[:], pD[0:128, 0:32], AF.Sigmoid),
                  reads=[bPD], writes=[bprb])
            for tc in range(4):
                pr = prb[:, tc * 8:(tc + 1) * 8]
                m1 = tpb[:, tc * 16:tc * 16 + 1]
                ge1 = tpb[:, tc * 16 + 2:tc * 16 + 10]
                cnt = tpb[:, tc * 16 + 10:tc * 16 + 11]
                strict = tpb[:, tc * 16 + 11:tc * 16 + 12]
                cnt2 = tpb[:, tc * 16 + 12:tc * 16 + 13]
                dd = tpb[:, tc * 16 + 13:tc * 16 + 14]
                th = tpb[:, tc * 16 + 14:tc * 16 + 15]
                fw.op("vector", lambda v, pr=pr, m1=m1: v.reduce_max(m1, pr, axis=AX.X), reads=[bprb], writes=[btpb])
                fw.op("vector", lambda v, pr=pr, ge1=ge1, m1=m1: v.tensor_scalar(ge1, pr, m1, None, ALU.is_ge),
                      reads=[bprb, btpb], writes=[btpb])
                fw.op("vector", lambda v, ge1=ge1, cnt=cnt: v.reduce_sum(cnt, ge1, axis=AX.X), reads=[btpb], writes=[btpb])
                fw.op("vector", lambda v, pr=pr, ge1=ge1, tc=tc: v.scalar_tensor_tensor(
                    ctk[:, tc * 8:(tc + 1) * 8], ge1, -1e30, pr, ALU.mult, ALU.add),
                    reads=[bprb, btpb], writes=[bctk])
                fw.op("vector", lambda v, tc=tc, strict=strict: v.reduce_max(strict, ctk[:, tc * 8:(tc + 1) * 8], axis=AX.X),
                      reads=[bctk], writes=[btpb])
                fw.op("vector", lambda v, cnt=cnt, cnt2=cnt2: v.tensor_scalar(cnt2, cnt, 2.0, None, ALU.is_ge),
                      reads=[btpb], writes=[btpb])
                fw.op("vector", lambda v, m1=m1, strict=strict, dd=dd: v.tensor_sub(dd, m1, strict),
                      reads=[btpb], writes=[btpb])
                fw.op("vector", lambda v, cnt2=cnt2, dd=dd, th=th: v.scalar_tensor_tensor(
                    th, cnt2, 1.0, dd, ALU.mult, ALU.mult), reads=[btpb], writes=[btpb])
                fw.op("vector", lambda v, th=th, strict=strict: v.tensor_add(th, th, strict),
                      reads=[btpb], writes=[btpb])
                fw.op("vector", lambda v, pr=pr, th=th, ge1=ge1: v.tensor_scalar(ge1, pr, th, None, ALU.is_ge),
                      reads=[bprb, btpb], writes=[btpb])
                fw.op("vector", lambda v, pr=pr, ge1=ge1, tc=tc: v.tensor_mul(ctk[:, tc * 8:(tc + 1) * 8], pr, ge1),
                      reads=[bprb, btpb], writes=[bctk])
            fw.op("vector", lambda v: v.tensor_copy(ctb[:], ctk[:]), reads=[bctk], writes=[bctb])

            def emit_cwf():
                for tc in range(4):
                    fw.op("tensor", lambda t, tc=tc: t.transpose(
                        pT[0:8, tc * 128:tc * 128 + 128], ctb[:, tc * 8:(tc + 1) * 8], cstb[:, C_ID:C_ID + 128]),
                        reads=[bctb, bcst], writes=[bPT])
                for tc in range(4):
                    fw.op("vector", lambda v, tc=tc: v.tensor_copy(
                        cwfb[:, tc * 128:(tc + 1) * 128], pT[0:8, tc * 128:tc * 128 + 128]),
                        reads=[bPT], writes=[bcwfb])

            def emit_bc(e):
                sl_ = 2 * rj + e % 2
                pp, bpp = (pE, bPE) if e % 2 == 0 else (pF, bPF)
                fw.op("tensor", lambda t, e=e, pp=pp: t.matmul(
                    pp[:, 0:COLS], cstb[0:8, C_SELE + e * 128:C_SELE + (e + 1) * 128],
                    cwfb[:], start=True, stop=True),
                    reads=[bcst, bcwfb], writes=[bpp])
                fw.op("scalar", lambda s, sl_=sl_, pp=pp: s.activation(
                    cwb[:, sl_ * COLS:(sl_ + 1) * COLS], pp[:, 0:COLS], AF.Copy),
                    reads=[bpp], writes=[bcwb[sl_]])
            return emit_cwf, emit_bc

        def ffn_multi(passes):
            emits = []
            for p in passes:
                if p.get("pre") is not None:
                    p["pre"]()
                emits.append(routing(p["src"], p["bsrc"], p["rj"]))
            experts = [("gs", "us", "ds", None)] + [(f"g{e}", f"u{e}", f"d{e}", e) for e in range(NE)]
            np_ = len(passes)
            for (gn, un, dn, e) in experts:
                grp = 0
                for sc4 in range(4):
                    sl_g, bs_g, _ = stream(gn, sc4 * 4096, (sc4 + 1) * 4096)
                    sl_u, bs_u, _ = stream(un, sc4 * 4096, (sc4 + 1) * 4096)
                    for mcl in range(4):
                        mc = sc4 * 4 + mcl
                        for pi, p in enumerate(passes):
                            src, bsrc = p["src"], p["bsrc"]
                            hbuf, bhbuf = p["h"], p["bh"]
                            pg, bpg = (pA, bPA) if grp % 2 == 0 else (pC, bPC)
                            pu, bpu = (pB, bPB) if grp % 2 == 0 else (pD, bPD)
                            for kc in range(DC):
                                if kc == DC - 1:
                                    fw.prewait("tensor", reads=[bs_u], writes=[bpu])
                                fw.op("tensor", lambda t, sl=sl_g, mcl=mcl, kc=kc, pg=pg, src=src: t.matmul(
                                    pg[:, 0:COLS],
                                    sl[:, mcl * D + kc * 128: mcl * D + kc * 128 + 128],
                                    src[:, kc * COLS:(kc + 1) * COLS],
                                    start=(kc == 0), stop=(kc == DC - 1)),
                                    reads=[bs_g, bsrc[kc]], writes=[bpg], signal=(kc == DC - 1))
                            sgb, bsgb = sg[grp % 2], bsg[grp % 2]
                            fw.op("scalar", lambda s, sgb=sgb, pg=pg: s.activation(sgb[:], pg[:, 0:COLS], AF.Silu),
                                  reads=[bpg], writes=[bsgb])
                            last_gu = (mcl == 3 and pi == np_ - 1)
                            for kc in range(DC):
                                if kc == DC - 1 and not last_gu:
                                    pgn = bPA if (grp + 1) % 2 == 0 else bPC
                                    fw.prewait("tensor", writes=[pgn])
                                fw.op("tensor", lambda t, sl=sl_u, mcl=mcl, kc=kc, pu=pu, src=src: t.matmul(
                                    pu[:, 0:COLS],
                                    sl[:, mcl * D + kc * 128: mcl * D + kc * 128 + 128],
                                    src[:, kc * COLS:(kc + 1) * COLS],
                                    start=(kc == 0), stop=(kc == DC - 1)),
                                    reads=[bs_u, bsrc[kc]], writes=[bpu], signal=(kc == DC - 1))
                            if e is None:
                                fw.op("vector", lambda v, mc=mc, sgb=sgb, pu=pu, hbuf=hbuf: v.tensor_mul(
                                    hbuf[:, mc * COLS:(mc + 1) * COLS], sgb[:], pu[:, 0:COLS]),
                                    reads=[bsgb, bpu], writes=[bhbuf[mc]])
                            else:
                                sc2, bsc2 = scr[grp % 2], bscr[grp % 2]
                                cslot = 2 * p["rj"] + (e % 2)
                                fw.op("vector", lambda v, sgb=sgb, sc2=sc2, pu=pu: v.tensor_mul(sc2[:], sgb[:], pu[:, 0:COLS]),
                                      reads=[bsgb, bpu], writes=[bsc2])
                                fw.op("vector", lambda v, mc=mc, sc2=sc2, hbuf=hbuf, cslot=cslot: v.tensor_mul(
                                    hbuf[:, mc * COLS:(mc + 1) * COLS], sc2[:],
                                    cwb[:, cslot * COLS:(cslot + 1) * COLS]),
                                    reads=[bsc2, bcwb[cslot]], writes=[bhbuf[mc]])
                            grp += 1
                if e is None:
                    for (emit_cwf, emit_bc) in emits:
                        emit_cwf()
                        emit_bc(0)
                        emit_bc(1)
                elif e + 2 < NE:
                    for (emit_cwf, emit_bc) in emits:
                        emit_bc(e + 2)
                dgrp = 0
                for sc2_ in range(4):
                    sl, bs, _ = stream(dn, sc2_ * 4096, (sc2_ + 1) * 4096)
                    for mcl in range(2):
                        oc = sc2_ * 2 + mcl
                        for pi, p in enumerate(passes):
                            hbuf, bhbuf = p["h"], p["bh"]
                            pp, bpp = (pE, bPE) if dgrp % 2 == 0 else (pF, bPF)
                            last_d = (oc == 7 and pi == np_ - 1)
                            for kc in range(IC):
                                if kc == IC - 1 and not last_d:
                                    ppn = bPE if (dgrp + 1) % 2 == 0 else bPF
                                    fw.prewait("tensor", writes=[ppn])
                                fw.op("tensor", lambda t, sl=sl, mcl=mcl, kc=kc, pp=pp, hbuf=hbuf: t.matmul(
                                    pp[:, 0:COLS],
                                    sl[:, mcl * I + kc * 128: mcl * I + kc * 128 + 128],
                                    hbuf[:, kc * COLS:(kc + 1) * COLS],
                                    start=(kc == 0), stop=(kc == IC - 1)),
                                    reads=[bs, bhbuf[kc]], writes=[bpp], signal=(kc == IC - 1))
                            yb, byb = p["yb"], p["byb"]
                            fw.op("vector", lambda v, oc=oc, pp=pp, yb=yb: v.scalar_tensor_tensor(
                                yb[:, oc * COLS:(oc + 1) * COLS], pp[:, 0:COLS], 0.5,
                                yb[:, oc * COLS:(oc + 1) * COLS], ALU.mult, ALU.add),
                                reads=[bpp, byb[oc]], writes=[byb[oc]])
                            if e == NE - 1 and p.get("post_chunk") is not None:
                                p["post_chunk"](oc)
                            dgrp += 1

        def fm_proj(name, src, bsrc, dst, bdst, mc_range, bias_off, scale=1.0):
            off, K, N = W_OFFS[name]
            KC = K // 128
            mc0 = mc_range[0]
            per_slot = 4096 // K
            idx = 0
            while idx < len(mc_range):
                group = mc_range[idx:idx + per_slot]
                sl, bs, _ = stream(name, group[0] * K, (group[-1] + 1) * K)
                for gi, mc in enumerate(group):
                    pp, bpp = (pA, bPA) if (mc % 2 == 0) else (pB, bPB)
                    for kc in range(KC):
                        fw.op("tensor", lambda t, sl=sl, gi=gi, kc=kc, pp=pp, K=K: t.matmul(
                            pp[:, 0:COLS],
                            sl[:, gi * K + kc * 128: gi * K + kc * 128 + 128],
                            src[:, kc * COLS:(kc + 1) * COLS],
                            start=(kc == 0), stop=False),
                            reads=[bs, bsrc[kc]], writes=[bpp], signal=False)
                    fw.op("tensor", lambda t, mc=mc, pp=pp: t.matmul(
                        pp[:, 0:COLS], bias_bf[0:1, bias_off + mc * 128: bias_off + mc * 128 + 128],
                        ones1b[0:1, 0:COLS], start=False, stop=True),
                        reads=[bbias, bones1b], writes=[bpp])
                    dc_ = mc - mc0
                    fw.op("scalar", lambda s, dc_=dc_, pp=pp, scale=scale: s.activation(
                        dst[:, dc_ * COLS:(dc_ + 1) * COLS], pp[:, 0:COLS], AF.Copy, scale=scale),
                        reads=[bpp], writes=[bdst if not isinstance(bdst, list) else bdst[dc_]])
                idx += per_slot

        def attn(i, pre=None, post_chunk=None, mid=None, mid2=None):
            src, bsrc = lnB, blnB
            if pre is None:
                ln_pass(i, src, bsrc)
            else:
                pre()
            fm_proj("qkv", src, bsrc, qb, bqb_l, list(range(0, 8)), B_OFFS["qkv"], scale=0.125)
            fm_proj("qkv", src, bsrc, kb, bkb_l, list(range(8, 16)), B_OFFS["qkv"])
            for half in range(2):
                sl, bs, _ = stream("qkv", (16 + 4 * half) * D, (20 + 4 * half) * D)
                for tc in range(4):
                    pp, bpp = (pA, bPA) if tc % 2 == 0 else (pB, bPB)
                    for kc in range(DC):
                        fw.op("tensor", lambda t, sl=sl, tc=tc, kc=kc, pp=pp: t.matmul(
                            pp[:, 0:512],
                            src[:, kc * COLS + tc * 128: kc * COLS + tc * 128 + 128],
                            sl.rearrange("p (mc k) -> p mc k", k=D)[:, :, kc * 128:kc * 128 + 128],
                            start=(kc == 0), stop=False),
                            reads=[bsrc[kc], bs], writes=[bpp], signal=False)
                    fw.op("tensor", lambda t, half=half, tc=tc, pp=pp: t.matmul(
                        pp[:, 0:512], ones1b[0:1, 0:128],
                        bias_bf[0:1, B_OFFS["qkv"] + 2048 + half * 512: B_OFFS["qkv"] + 2048 + half * 512 + 512],
                        start=False, stop=True),
                        reads=[bones1b, bbias], writes=[bpp])
                    fw.op("scalar", lambda s, half=half, tc=tc, pp=pp: s.activation(
                        vb[:, tc * D + half * 512: tc * D + half * 512 + 512], pp[:, 0:512], AF.Copy),
                        reads=[bpp], writes=[bvb])
            fm_proj("rq", src, bsrc, rqb, brqb, [0, 1], B_OFFS["rq"], scale=0.125)
            for ch in range(DC):
                for s_ in range(SEQ):
                    dst = pooled[:, ch * 128 + s_ * 64 + 4 * i: ch * 128 + s_ * 64 + 4 * i + 4]
                    srcv = src.ap().rearrange("p (c8 s2 a b) -> p c8 s2 a b", c8=DC, s2=SEQ, a=4)[:, ch, s_]
                    sc2, bsc2 = scr[ch % 2], bscr[ch % 2]
                    fw.op("vector", lambda v, srcv=srcv, sc2=sc2: v.reduce_sum(
                        sc2.ap().rearrange("p (a b) -> p a b", a=4)[:, :, 0:1], srcv, axis=AX.X),
                        reads=[bsrc[ch]], writes=[bsc2])
                    fw.op("vector", lambda v, dst=dst, sc2=sc2: v.tensor_copy(dst, sc2[:, 0:4]),
                        reads=[bsc2], writes=[bpooled])
            sl, bs, _ = stream("rkv", 0, 4096)
            for mc in range(4):
                pp, bpp = (pA, bPA) if mc % 2 == 0 else (pB, bPB)
                for kc in range(DC):
                    fw.op("tensor", lambda t, sl=sl, mc=mc, kc=kc, pp=pp: t.matmul(
                        pp[:, 0:128],
                        sl[:, mc * D + kc * 128: mc * D + kc * 128 + 128],
                        pooled[:, kc * 128:(kc + 1) * 128],
                        start=(kc == 0), stop=(kc == DC - 1)),
                        reads=[bs, bpooled], writes=[bpp], signal=(kc == DC - 1))
                dstb, bdstb = (rkbuf, brk) if mc < 2 else (rvfbuf, brvf)
                fw.op("vector", lambda v, mc=mc, pp=pp, dstb=dstb: v.tensor_scalar(
                    dstb[:, (mc % 2) * 128:(mc % 2) * 128 + 128], pp[:, 0:128],
                    1.0 / 64, cstf_s[:, mc:mc + 1], ALU.mult, ALU.add),
                    reads=[bpp, bcstf], writes=[bdstb])
            for s_ in range(SEQ):
                for hh in range(4):
                    fw.op("tensor", lambda t, s_=s_, hh=hh, hb=64 * (hh % 2): t.transpose(
                        pT[0:64, 0:64],
                        rvfbuf[hb:hb + 64, (hh // 2) * 128 + s_ * 64: (hh // 2) * 128 + s_ * 64 + 64],
                        cstb[hb:hb + 64, C_ID + hb:C_ID + hb + 64]),
                        reads=[brvf, bcst], writes=[bPT])
                    fw.op("vector", lambda v, s_=s_, hh=hh: v.tensor_copy(
                        rvlat[64 * s_:64 * s_ + 64, hh * 64:hh * 64 + 64], pT[0:64, 0:64]),
                        reads=[bPT], writes=[brvlat])
            sc_banks = [(pA, bPA), (pB, bPB), (pE, bPE), (pF, bPF)]
            tr_banks = [(pT, bPT), (pT2, bPT2)]
            out_banks = [(pC, bPC), (pD, bPD)]
            groups = [(s_, qc, hh) for s_ in range(SEQ) for qc in range(2) for hh in range(NH)]
            stages = []
            for gidx, (s_, qc, hh) in enumerate(groups):
                first = (i == 0 and qc == 0)
                rot = gidx % 4
                psc, bpsc = sc_banks[rot]
                ptr, bptr = tr_banks[gidx % 2]
                pout, bpout = out_banks[gidx % 2]
                pbo = pbf[:, rot * 256: rot * 256 + 256]
                ato = attnT[:, rot * 256: rot * 256 + 256]
                bpbo, bato = bpbf[rot], battnT[rot]
                mxo, deno, rdeno = mx[:, rot:rot + 1], den[:, rot:rot + 1], rden[:, rot:rot + 1]
                bmxo, bdeno, brdeno = bmx[rot], bden[rot], brden[rot]
                hch, hbase = hh // 2, 64 * (hh % 2)
                qa = qb[hbase:hbase + 64, hch * COLS + s_ * TP + qc * 128: hch * COLS + s_ * TP + qc * 128 + 128]
                W_ = 128 if first else 256
                nkc = 1 if first else 2
                trc = (rot // 2) * 512

                def S0(first=first, qa=qa, psc=psc, bpsc=bpsc, hch=hch, hbase=hbase, s_=s_, qc=qc):
                    if first:
                        fw.op("tensor", lambda t: t.matmul(
                            psc[:, 0:128], qa,
                            kb[hbase:hbase + 64, hch * COLS + s_ * TP: hch * COLS + s_ * TP + 128],
                            start=True, stop=False, skip_group_check=True),
                            reads=[blnF1[hch], bkb], writes=[bpsc], signal=False)
                        fw.op("tensor", lambda t: t.matmul(
                            psc[:, 0:128], cstb[:, C_ID:C_ID + 128], cstb[:, C_M128:C_M128 + 128],
                            start=False, stop=True, skip_group_check=True),
                            reads=[bcst], writes=[bpsc])
                    else:
                        ksrc0 = (ktail[hbase:hbase + 64, hch * SEQ * 128 + s_ * 128: hch * SEQ * 128 + s_ * 128 + 128]
                                 if qc == 0 else
                                 kb[hbase:hbase + 64, hch * COLS + s_ * TP: hch * COLS + s_ * TP + 128])
                        bks0 = bktail if qc == 0 else bkb
                        ksrc1 = kb[hbase:hbase + 64, hch * COLS + s_ * TP + qc * 128: hch * COLS + s_ * TP + qc * 128 + 128]
                        fw.op("tensor", lambda t: t.matmul(
                            psc[:, 0:128], qa, ksrc0,
                            start=True, stop=False, skip_group_check=True),
                            reads=[blnF1[hch], bks0], writes=[bpsc], signal=False)
                        fw.op("tensor", lambda t: t.matmul(
                            psc[:, 128:256], qa, ksrc1,
                            start=False, stop=False, skip_group_check=True),
                            reads=[blnF1[hch], bkb], writes=[bpsc], signal=False)
                        fw.op("tensor", lambda t: t.matmul(
                            psc[:, 0:256], cstb[:, C_ID:C_ID + 128], cstb[:, C_M256:C_M256 + 256],
                            start=False, stop=True, skip_group_check=True),
                            reads=[bcst], writes=[bpsc])

                def S1(psc=psc, bpsc=bpsc, mxo=mxo, bmxo=bmxo, W_=W_):
                    fw.op("vector", lambda v: v.reduce_max(mxo, psc[:, 0:W_], axis=AX.X, negate=True),
                          reads=[bpsc], writes=[bmxo])

                def S2(psc=psc, bpsc=bpsc, pbo=pbo, bpbo=bpbo, mxo=mxo, bmxo=bmxo, deno=deno, bdeno=bdeno, W_=W_):
                    fw.op("scalar", lambda sx: sx.activation(pbo[:, 0:W_], psc[:, 0:W_], AF.Exp, bias=mxo, accum_out=deno),
                          reads=[bpsc, bmxo], writes=[bpbo, bdeno])

                def S3(pbo=pbo, bpbo=bpbo, deno=deno, bdeno=bdeno, rdeno=rdeno, brdeno=brdeno, W_=W_):
                    fw.op("vector", lambda v: v.reciprocal(rdeno, deno), reads=[bdeno], writes=[brdeno])
                    fw.op("vector", lambda v: v.tensor_scalar(pbo[:, 0:W_], pbo[:, 0:W_], rdeno, None, ALU.mult),
                          reads=[brdeno], writes=[bpbo])

                def S4(pbo=pbo, bpbo=bpbo, ptr=ptr, bptr=bptr, trc=trc, nkc=nkc):
                    for kcb in range(nkc):
                        fw.op("tensor", lambda t, kcb=kcb: t.transpose(
                            ptr[:, trc + kcb * 128: trc + kcb * 128 + 128],
                            pbo[:, kcb * 128:kcb * 128 + 128],
                            cstb[:, C_ID:C_ID + 128]),
                            reads=[bpbo, bcst], writes=[bptr])

                def S5(ato=ato, bato=bato, ptr=ptr, bptr=bptr, trc=trc, nkc=nkc):
                    for kcb in range(nkc):
                        if kcb == 0:
                            fw.op("vector", lambda v, kcb=kcb: v.tensor_copy(
                                ato[:, kcb * 128:kcb * 128 + 128],
                                ptr[:, trc + kcb * 128: trc + kcb * 128 + 128]),
                                reads=[bptr], writes=[bato])
                        else:
                            fw.op("scalar", lambda sx, kcb=kcb: sx.activation(
                                ato[:, kcb * 128:kcb * 128 + 128],
                                ptr[:, trc + kcb * 128: trc + kcb * 128 + 128], AF.Copy),
                                reads=[bptr], writes=[bato])

                def S6(first=first, s_=s_, qc=qc, hh=hh, ato=ato, bato=bato, pout=pout, bpout=bpout, nkc=nkc):
                    for kcb in range(nkc):
                        if first:
                            vsrc, bvsrc = vb[:, (s_ * 2) * D + hh * 64:(s_ * 2) * D + hh * 64 + 64], bvb
                        elif qc == 0 and kcb == 0:
                            vsrc, bvsrc = vtail[:, s_ * D + hh * 64: s_ * D + hh * 64 + 64], bvtail
                        else:
                            tc = s_ * 2 + (qc - 1 + kcb)
                            vsrc, bvsrc = vb[:, tc * D + hh * 64: tc * D + hh * 64 + 64], bvb
                        fw.op("tensor", lambda t, vsrc=vsrc, bvsrc=bvsrc, kcb=kcb: t.matmul(
                            pout[0:64, 0:128], vsrc, ato[:, kcb * 128:kcb * 128 + 128],
                            start=(kcb == 0), stop=(kcb == nkc - 1)),
                            reads=[bvsrc, bato], writes=[bpout], signal=(kcb == nkc - 1))

                def S7(hch=hch, hbase=hbase, s_=s_, qc=qc, pout=pout, bpout=bpout):
                    fw.op("scalar", lambda sx: sx.activation(
                        h2[hbase:hbase + 64, hch * COLS + s_ * TP + qc * 128: hch * COLS + s_ * TP + qc * 128 + 128],
                        pout[0:64, 0:128], AF.Copy),
                        reads=[bpout], writes=[blocal[hch]])

                stages.append([S0, S1, S2, S3, S4, S5, S6, S7])
            NS = 8
            for step in range(len(stages) + NS - 1):
                for st in range(NS - 1, -1, -1):
                    g = step - st
                    if 0 <= g < len(stages):
                        stages[g][st]()
            if i == 0:
                for s_ in range(SEQ):
                    for ch in range(DC):
                        fw.op("vector", lambda v, s_=s_, ch=ch: v.memset(
                            h2[:, ch * COLS + s_ * TP: ch * COLS + s_ * TP + 1], 0.0),
                            writes=[blocal[ch]])
            lgroups = [(s_, qc, hh) for s_ in range(SEQ) for qc in range(2) for hh in range(4)]
            lstages = []
            for gidx, (s_, qc, hh) in enumerate(lgroups):
                Cq = 4 * i + 2 * qc
                rot = gidx % 4
                psc, bpsc = sc_banks[rot]
                ptr, bptr = tr_banks[gidx % 2]
                pout, bpout = out_banks[gidx % 2]
                pbo = pbf[:, rot * 256: rot * 256 + 64]
                bpbo = bpbf[rot]
                mxo, deno, rdeno = mx[:, 4 + rot:5 + rot], den[:, 4 + rot:5 + rot], rden[:, 4 + rot:5 + rot]
                bmxo, bdeno, brdeno = bmx[4 + rot], bden[4 + rot], brden[4 + rot]
                hch, hbase = hh // 2, 64 * (hh % 2)
                rqa = rqb[hbase:hbase + 64, hch * COLS + s_ * TP + qc * 128: hch * COLS + s_ * TP + qc * 128 + 128]
                trc = (rot // 2) * 512 + 256
                lslot = (rot % 2) * 128

                def L0(rqa=rqa, psc=psc, bpsc=bpsc, hch=hch, hbase=hbase, s_=s_, Cq=Cq):
                    fw.op("tensor", lambda t: t.matmul(
                        psc[:, 0:64], rqa,
                        rkbuf[hbase:hbase + 64, hch * 128 + s_ * 64: hch * 128 + s_ * 64 + 64],
                        start=True, stop=False, skip_group_check=True),
                        reads=[brqb, brk], writes=[bpsc], signal=False)
                    fw.op("tensor", lambda t: t.matmul(
                        psc[:, 0:64], cstb[:, C_ID:C_ID + 128],
                        cstb[:, C_LMASK + 64 - Cq: C_LMASK + 128 - Cq],
                        start=False, stop=True, skip_group_check=True),
                        reads=[bcst], writes=[bpsc])

                def L1(psc=psc, bpsc=bpsc, mxo=mxo, bmxo=bmxo):
                    fw.op("vector", lambda v: v.reduce_max(mxo, psc[:, 0:64], axis=AX.X, negate=True),
                          reads=[bpsc], writes=[bmxo])

                def L2(psc=psc, bpsc=bpsc, pbo=pbo, bpbo=bpbo, mxo=mxo, bmxo=bmxo, deno=deno, bdeno=bdeno):
                    fw.op("scalar", lambda sx: sx.activation(pbo, psc[:, 0:64], AF.Exp, bias=mxo, accum_out=deno),
                          reads=[bpsc, bmxo], writes=[bpbo, bdeno])

                def L3(pbo=pbo, bpbo=bpbo, deno=deno, bdeno=bdeno, rdeno=rdeno, brdeno=brdeno, qc=qc):
                    fw.op("vector", lambda v: v.reciprocal(rdeno, deno), reads=[bdeno], writes=[brdeno])
                    fw.op("vector", lambda v: v.tensor_scalar(pbo, pbo, rdeno, None, ALU.mult),
                          reads=[brdeno], writes=[bpbo])
                    if i == 0 and qc == 0:
                        fw.op("vector", lambda v: v.memset(pbo[0:64, :], 0.0), writes=[bpbo])

                def L4(pbo=pbo, bpbo=bpbo, ptr=ptr, bptr=bptr, trc=trc):
                    fw.op("tensor", lambda t: t.transpose(
                        ptr[0:64, trc: trc + 128], pbo, cstb[:, C_ID:C_ID + 128]),
                        reads=[bpbo, bcst], writes=[bptr])

                def L5(s_=s_, ptr=ptr, bptr=bptr, trc=trc, lslot=lslot):
                    fw.op("vector", lambda v: v.tensor_copy(
                        lattnT[64 * s_:64 * s_ + 64, lslot:lslot + 128],
                        ptr[0:64, trc: trc + 128]),
                        reads=[bptr], writes=[blattnT])

                def L6(s_=s_, hh=hh, lslot=lslot, pout=pout, bpout=bpout):
                    fw.op("tensor", lambda t: t.matmul(
                        pout[0:64, 128:256],
                        rvlat[64 * s_:64 * s_ + 64, hh * 64:hh * 64 + 64],
                        lattnT[64 * s_:64 * s_ + 64, lslot:lslot + 128],
                        start=True, stop=True, skip_group_check=True),
                        reads=[brvlat, blattnT], writes=[bpout])

                def L7(hch=hch, hbase=hbase, s_=s_, qc=qc, pout=pout, bpout=bpout):
                    fw.op("scalar", lambda sx: sx.activation(
                        latout[hbase:hbase + 64, hch * COLS + s_ * TP + qc * 128: hch * COLS + s_ * TP + qc * 128 + 128],
                        pout[0:64, 128:256], AF.Copy),
                        reads=[bpout], writes=[blatout])

                lstages.append([L0, L1, L2, L3, L4, L5, L6, L7])
            for step in range(len(lstages) + NS - 1):
                for st in range(NS - 1, -1, -1):
                    g = step - st
                    if 0 <= g < len(lstages):
                        lstages[g][st]()
            sl, bs, _ = stream("rout", 0, 2048)
            for mc in range(DC):
                pp, bpp = (pA, bPA) if mc % 2 == 0 else (pB, bPB)
                for kc in range(2):
                    fw.op("tensor", lambda t, sl=sl, mc=mc, kc=kc, pp=pp: t.matmul(
                        pp[:, 0:COLS], sl[:, mc * LD + kc * 128: mc * LD + kc * 128 + 128],
                        latout[:, kc * COLS:(kc + 1) * COLS],
                        start=(kc == 0), stop=False),
                        reads=[bs, blatout], writes=[bpp], signal=False)
                fw.op("tensor", lambda t, mc=mc, pp=pp: t.matmul(
                    pp[:, 0:COLS], bias_bf[0:1, B_OFFS["rout"] + mc * 128: B_OFFS["rout"] + mc * 128 + 128],
                    ones1b[0:1, 0:COLS], start=False, stop=True),
                    reads=[bbias, bones1b], writes=[bpp])
                fw.op("scalar", lambda s, mc=mc, pp=pp: s.activation(
                    h2[:, (DC + mc) * COLS:(DC + mc + 1) * COLS], pp[:, 0:COLS], AF.Copy),
                    reads=[bpp], writes=[bremote[mc]])
            if mid is not None:
                mid()
            gstages = []
            for ch in range(DC):
                pg_, bpg_ = (pE, bPE) if ch % 2 == 0 else (pF, bPF)
                pb_, bpb_ = (pC, bPC) if ch % 2 == 0 else (pD, bPD)
                gsl = ch % 2
                sc2, bsc2 = scr[ch % 2], bscr[ch % 2]

                def G0(ch=ch, pg_=pg_, bpg_=bpg_):
                    fw.op("tensor", lambda t: t.matmul(
                        pg_[0:2, 0:COLS], cstb[:, C_GL:C_GL + 2], h2[:, ch * COLS:(ch + 1) * COLS],
                        start=True, stop=False),
                        reads=[bcst, blocal[ch]], writes=[bpg_], signal=False)
                    fw.op("tensor", lambda t: t.matmul(
                        pg_[0:2, 0:COLS], cstb[:, C_GR:C_GR + 2], h2[:, (DC + ch) * COLS:(DC + ch + 1) * COLS],
                        start=False, stop=True),
                        reads=[bcst, bremote[ch]], writes=[bpg_])

                def G1(pg_=pg_, bpg_=bpg_, gsl=gsl):
                    fw.op("scalar", lambda sx: sx.activation(gatebf[:, gsl * COLS:(gsl + 1) * COLS], pg_[0:2, 0:COLS], AF.Sigmoid),
                          reads=[bpg_], writes=[bgate])

                def G2(pb_=pb_, bpb_=bpb_, gsl=gsl):
                    fw.op("tensor", lambda t: t.matmul(
                        pb_[:, 0:COLS], cstb[0:2, C_SEL2:C_SEL2 + 128], gatebf[:, gsl * COLS:(gsl + 1) * COLS],
                        start=True, stop=True),
                        reads=[bcst, bgate], writes=[bpb_])

                def G3(ch=ch, sc2=sc2, bsc2=bsc2):
                    fw.op("vector", lambda v: v.tensor_sub(
                        sc2[:], h2[:, ch * COLS:(ch + 1) * COLS], h2[:, (DC + ch) * COLS:(DC + ch + 1) * COLS]),
                        reads=[blocal[ch], bremote[ch]], writes=[bsc2])

                def G4(sc2=sc2, bsc2=bsc2, pb_=pb_, bpb_=bpb_):
                    fw.op("vector", lambda v: v.tensor_mul(sc2[:], sc2[:], pb_[:, 0:COLS]),
                          reads=[bsc2, bpb_], writes=[bsc2])

                def G5(ch=ch, sc2=sc2, bsc2=bsc2):
                    fw.op("vector", lambda v: v.tensor_add(
                        lnB[:, ch * COLS:(ch + 1) * COLS], sc2[:], h2[:, (DC + ch) * COLS:(DC + ch + 1) * COLS]),
                        reads=[bsc2, bremote[ch]], writes=[blnB[ch]])

                gstages.append([G0, G1, G2, G3, G4, G5])
            for step in range(len(gstages) + 5):
                for st in range(5, -1, -1):
                    g_ = step - st
                    if 0 <= g_ < len(gstages):
                        gstages[g_][st]()
            if mid2 is not None:
                mid2()
            yb, byb = y[i % NYB], bY[i % NYB]
            for half in range(2):
                sl, bs, _ = stream("outw", half * 4096, (half + 1) * 4096)
                for mcl in range(4):
                    mc = half * 4 + mcl
                    pp, bpp = (pC, bPC) if mc % 2 == 0 else (pD, bPD)
                    for kc in range(DC):
                        fw.op("tensor", lambda t, sl=sl, mcl=mcl, kc=kc, pp=pp: t.matmul(
                            pp[:, 0:COLS], sl[:, mcl * D + kc * 128: mcl * D + kc * 128 + 128],
                            lnB[:, kc * COLS:(kc + 1) * COLS],
                            start=(kc == 0), stop=False),
                            reads=[bs, blnB[kc]], writes=[bpp], signal=False)
                    fw.op("tensor", lambda t, mc=mc, pp=pp: t.matmul(
                        pp[:, 0:COLS], bias_bf[0:1, B_OFFS["outw"] + mc * 128: B_OFFS["outw"] + mc * 128 + 128],
                        ones1b[0:1, 0:COLS], start=False, stop=True),
                        reads=[bbias, bones1b], writes=[bpp])
                    fw.op("vector", lambda v, mc=mc, pp=pp: v.tensor_add(
                        yb[:, mc * COLS:(mc + 1) * COLS], yb[:, mc * COLS:(mc + 1) * COLS], pp[:, 0:COLS]),
                        reads=[bpp, byb[mc]], writes=[byb[mc]])
                    if post_chunk is not None:
                        post_chunk(mc)
            fw.op("vector", lambda v: v.tensor_copy(
                ktail.ap().rearrange("p (c s k) -> p c s k", c=DC, s=SEQ),
                kb.ap().rearrange("p (c s pp) -> p c s pp", c=DC, s=SEQ)[:, :, :, 128:256]),
                reads=[bkb], writes=[bktail])
            fw.op("vector", lambda v: v.tensor_copy(
                vtail.ap().rearrange("p (s d) -> p s d", s=SEQ),
                vb.ap().rearrange("p (s two d) -> p s two d", s=SEQ, two=2)[:, :, 1]),
                reads=[bvb], writes=[bvtail])

        bqb_l = blnF1
        bkb_l = [bkb] * DC

        def mk_passA(i):
            yb, byb = y[i % NYB], bY[i % NYB]
            off = i * DC * COLS

            def store(ch, yb=yb, byb=byb, off=off):
                fw.op("gpd", lambda g, ch=ch: g.dma_start(
                    out=y_d[:, off + ch * COLS: off + (ch + 1) * COLS],
                    in_=yb[:, ch * COLS:(ch + 1) * COLS]),
                    reads=[byb[ch]], writes=[bydram], dma=True)
            return {"src": lnA, "bsrc": blnA, "yb": yb, "byb": byb,
                    "h": h, "bh": bh, "rj": 0, "post_chunk": store,
                    "pre": lambda i=i: ln_p2(i, lnA, blnA)}

        def mk_passB(i):
            return {"src": lnF1, "bsrc": blnF1, "yb": y[i % NYB], "byb": bY[i % NYB],
                    "h": h2, "bh": bh2, "rj": 1,
                    "post_chunk": lambda ch, i=i: ln_p1(i, lnB, blnB, ch)}

        load_x(0)
        ln_pass(0, lnF1, blnF1)
        ffn_multi([mk_passB(0)])
        for i in range(NT):
            if i + 1 < NT:
                load_x(i + 1)

            def mid(i=i):
                if i + 1 < NT:
                    for ch in range(DC):
                        ln_p1(i + 1, lnF1, blnF1, ch)

            def mid2(i=i):
                if i + 1 < NT:
                    ln_p2(i + 1, lnF1, blnF1)
            attn(i, pre=lambda i=i: ln_p2(i, lnB, blnB),
                 post_chunk=lambda ch, i=i: ln_p1(i, lnA, blnA, ch),
                 mid=mid, mid2=mid2)
            if i == 0:
                slot_state["f32"] = False
            plist = []
            if i + 1 < NT:
                plist.append(mk_passB(i + 1))
            plist.append(mk_passA(i))
            ffn_multi(plist)

        fw.finalize(stack)
    return nc


def kernel(**inputs):
    in_maps = _host_layouts(inputs)
    nc = build_program(_PHASES)
    res = run_bass_kernel_spmd(nc, in_maps, core_ids=list(range(NCORES)))
    return _unshard_output(res)
